# version 08
# baseline (speedup 1.0000x reference)
import sys
sys.path.insert(0, "/opt/trn_rl_repo")
import os
import contextlib
import numpy as np
import ml_dtypes
import concourse.bass as bass
import concourse.mybir as mybir
from concourse.bass_utils import run_bass_kernel_spmd

F32 = mybir.dt.float32
BF16 = mybir.dt.bfloat16
AF = mybir.ActivationFunctionType
OP = mybir.AluOpType

B, T, C = 2, 8192, 1024
NCORE = 8
TLOC = 2048
H0, DH0 = 16, 64
SHIFT0 = 20.0
QH = 512
WIN = 256
FFN_DEPTH = 6
SLOT = 66

ENGINES = ("sync", "gpsimd", "tensor", "vector", "scalar")


class Res:
    __slots__ = ("wl", "r", "name")

    def __init__(self, name=""):
        self.wl = []
        self.r = []
        self.name = name


NQDMA = 20
STRICT = int(os.environ.get("STRICT", "0"))


class P:
    def __init__(self, nc, ctx):
        self.nc = nc
        self.ops = {e: [] for e in ENGINES}
        self.sem = {}
        for e in ("tensor", "vector", "scalar"):
            self.sem[e] = ctx.enter_context(nc.semaphore("sem_" + e))
        self.pool = {}
        for e in ("sync", "gpsimd"):
            self.pool[e] = [ctx.enter_context(nc.semaphore(f"sem_{e}{i}"))
                            for i in range(NQDMA)]
            for i in range(NQDMA):
                self.sem[f"{e}#{i}"] = self.pool[e][i]
        self.cnt = {e: 0 for e in ("tensor", "vector", "scalar")}
        self.dmacnt = {"sync": 0, "gpsimd": 0}
        self.uses = {f"{e}#{i}": 0 for e in ("sync", "gpsimd") for i in range(NQDMA)}
        self.wm = {}

    def _wait(self, eng, key, val):
        if (key == eng == "tensor") or val <= 0:
            return
        wk = (eng, key)
        if val <= self.wm.get(wk, 0):
            return
        self.wm[wk] = val
        sem = self.sem[key]
        self.ops[eng].append(lambda h, sem=sem, val=val: h.wait_ge(sem, val))

    def op(self, eng, fn, reads=(), writes=(), amt=None):
        if STRICT:
            for e in ENGINES:
                self.barrier(e)
        if eng in ("sync", "gpsimd"):
            k = self.dmacnt[eng] % NQDMA
            self.dmacnt[eng] += 1
            key = f"{eng}#{k}"
            self.uses[key] += 1
            if self.uses[key] > 1:
                self._wait(eng, key, 16 * (self.uses[key] - 1))
            tokval = 16 * self.uses[key]
            amt = 16
        else:
            key = eng
            self.cnt[eng] += (amt if amt is not None else 1)
            if amt is None:
                amt = 1
            tokval = self.cnt[eng]
        for r in reads:
            for wt in r.wl:
                self._wait(eng, *wt)
        for w in writes:
            for rd in w.r:
                self._wait(eng, rd[0], rd[1])
        tok = (key, tokval)
        sem = self.sem[key]
        if amt:
            def wrapped(h, fn=fn, sem=sem, amt=amt):
                ins = fn(h)
                ins.then_inc(sem, amt)
        else:
            def wrapped(h, fn=fn):
                fn(h)
        self.ops[eng].append(wrapped)
        for r in reads:
            r.r.append(tok)
        for w in writes:
            w.wl.append(tok)
        return tok

    def barrier(self, eng, others=None):
        for o in (others or ENGINES):
            if o == eng:
                continue
            if o in ("sync", "gpsimd"):
                for i in range(NQDMA):
                    key = f"{o}#{i}"
                    self._wait(eng, key, 16 * self.uses[key])
            else:
                self._wait(eng, o, self.cnt[o])

    def final_waits(self):
        for e in ("sync", "gpsimd"):
            for i in range(NQDMA):
                key = f"{e}#{i}"
                val = 16 * self.uses[key]
                if val:
                    sem = self.sem[key]
                    self.ops[e].append(
                        lambda h, sem=sem, val=val: h.wait_ge(sem, val))

    def finalize(self, block):
        for e in ENGINES:
            ops = self.ops[e]

            def body(h, ops=ops):
                for f in ops:
                    f(h)
            getattr(block, e)(body)


def _r(*names):
    return [Res(n) for n in names]


def emit_consts(p, E, nc):
    t = {}
    t["ones_b"] = E(nc.sbuf_tensor("ones_b", [128, 1], BF16))
    t["ones_f"] = E(nc.sbuf_tensor("ones_f", [1, 128], F32))
    t["csts"] = E(nc.sbuf_tensor("csts", [128, 8], F32))
    r = Res("consts")
    p.op("vector", lambda h: h.memset(t["ones_b"][:], 1.0), writes=[r])
    p.op("vector", lambda h: h.memset(t["ones_f"][:], 1.0), writes=[r])
    p.op("vector", lambda h: h.memset(t["csts"][:, 0:1], -SHIFT0), writes=[r])
    p.op("vector", lambda h: h.memset(t["csts"][:, 1:2], 1e-5), writes=[r])
    p.op("vector", lambda h: h.memset(t["csts"][:, 2:3], 8.0), writes=[r])
    t["consts_res"] = r
    return t


def emit_ln_chunk(p, cst, width, src_bf, r_src_bf, dst_bf, r_dst_bf,
                  psr, sbufs, src_f32=None, r_src_f32=None,
                  dst_f32=None, r_dst_f32=None, sq=None, r_sq=None,
                  gb=None, r_gb=None):
    ones_b, ones_f, csts = cst["ones_b"], cst["ones_f"], cst["csts"]
    rowbuf, r_rowbuf = sbufs["rowbuf"], sbufs["r_rowbuf"]
    W = width

    def sl(buf, cc):
        return buf(cc) if callable(buf) else buf[:, cc * W:(cc + 1) * W]

    for cc in range(8):
        p.op("scalar", lambda h, cc=cc: h.activation(
            sq[:, cc * W:(cc + 1) * W], sl(src_bf, cc), AF.Square),
            reads=[r_src_bf], writes=[r_sq], amt=(1 if cc == 7 else 0))

    def mm_mean(h):
        last = None
        for cc in range(8):
            last = h.matmul(psr["row"][0][:, 0:W], ones_b[:], sl(src_bf, cc),
                            start=(cc == 0), stop=(cc == 7))
        return last
    p.op("tensor", mm_mean, reads=[r_src_bf, cst["consts_res"]], writes=[psr["row"][1]])

    def mm_sq(h):
        last = None
        for cc in range(8):
            last = h.matmul(psr["row2"][0][:, 0:W], ones_b[:],
                            sq[:, cc * W:(cc + 1) * W],
                            start=(cc == 0), stop=(cc == 7))
        return last
    p.op("tensor", mm_sq, reads=[r_sq], writes=[psr["row2"][1]])

    mean_r = rowbuf[0:1, 0:W]
    rstd_r = rowbuf[0:1, W:2 * W]
    var_r = rowbuf[0:1, 2 * W:3 * W]
    p.op("vector", lambda h: h.tensor_scalar_mul(mean_r, psr["row"][0][:, 0:W], 1.0 / C),
         reads=[psr["row"][1]], writes=[sbufs["r_rowbuf"]])
    p.op("vector", lambda h: h.tensor_tensor(var_r, mean_r, mean_r, OP.mult),
         reads=[r_rowbuf], writes=[r_rowbuf])
    p.op("vector", lambda h: h.scalar_tensor_tensor(
        var_r, psr["row2"][0][:, 0:W], 1.0 / C, var_r, OP.mult, OP.subtract),
        reads=[psr["row2"][1], r_rowbuf], writes=[r_rowbuf])
    p.op("scalar", lambda h: h.activation(rstd_r, var_r, AF.Sqrt, bias=csts[0:1, 1:2]),
         reads=[r_rowbuf, cst["consts_res"]], writes=[r_rowbuf])
    p.op("vector", lambda h: h.reciprocal(rstd_r, rstd_r),
         reads=[r_rowbuf], writes=[r_rowbuf])
    p.op("tensor", lambda h: h.matmul(psr["bc"][0][:, 0:W], ones_f[0:1, :], mean_r,
                                      start=True, stop=True),
         reads=[r_rowbuf, cst["consts_res"]], writes=[psr["bc"][1]])
    p.op("tensor", lambda h: h.matmul(psr["bc2"][0][:, 0:W], ones_f[0:1, :], rstd_r,
                                      start=True, stop=True),
         reads=[r_rowbuf], writes=[psr["bc2"][1]])
    xsrc, r_xs = (src_f32, r_src_f32) if src_f32 is not None else (src_bf, r_src_bf)
    tgt, r_tgt = (dst_f32, r_dst_f32) if dst_f32 is not None else (dst_bf, r_dst_bf)
    for cc in range(8):
        p.op("vector", lambda h, cc=cc: h.tensor_tensor(
            sl(tgt, cc), sl(xsrc, cc), psr["bc"][0][:, 0:W], OP.subtract),
            reads=[r_xs, psr["bc"][1]], writes=[r_tgt])
    for cc in range(8):
        p.op("vector", lambda h, cc=cc: h.tensor_tensor(
            sl(tgt, cc), sl(tgt, cc), psr["bc2"][0][:, 0:W], OP.mult),
            reads=[r_tgt, psr["bc2"][1]], writes=[r_tgt])
    if gb is not None:
        gcols, bcols = gb
        for cc in range(8):
            p.op("vector", lambda h, cc=cc: h.tensor_scalar(
                sl(tgt, cc), sl(tgt, cc), gcols[:, cc:cc + 1], bcols[:, cc:cc + 1],
                OP.mult, OP.add),
                reads=[r_tgt, r_gb], writes=[r_tgt])
    if dst_f32 is not None:
        for cc in range(8):
            p.op("vector", lambda h, cc=cc: h.tensor_copy(
                sl(dst_bf, cc), sl(dst_f32, cc)),
                reads=[r_dst_f32], writes=[r_dst_bf])


def w_lhsT(w_sb, cch, och, width=128):
    return w_sb[:, cch * C + och * 128: cch * C + och * 128 + width]


def build_launch_A(Tk=T, qext=TLOC + QH, qtot=TLOC, kch=512):
    nc = bass.Bass("TRN2", target_bir_lowering=False)
    nc.detect_race_conditions = False
    NKC = Tk // kch
    NQC = qext // kch
    QOUT = 512
    NQO = qtot // QOUT
    KSUB = 128
    NKS = Tk // KSUB
    KHC = 2048 if Tk >= 2048 else Tk
    NKH = Tk // KHC

    xb = nc.dram_tensor("xb", [C, Tk], F32, kind="ExternalInput").ap()
    xq = nc.dram_tensor("xq", [C, qext], F32, kind="ExternalInput").ap()
    wq_in = nc.dram_tensor("wqT", [C, C], F32, kind="ExternalInput").ap()
    wk_in = nc.dram_tensor("wkT", [C, C], F32, kind="ExternalInput").ap()
    wv_in = nc.dram_tensor("wvT", [C, C], F32, kind="ExternalInput").ap()
    wo_in = nc.dram_tensor("woT", [C, C], F32, kind="ExternalInput").ap()
    bq_in = nc.dram_tensor("bq", [128, 8], F32, kind="ExternalInput").ap()
    bk_in = nc.dram_tensor("bk", [128, 8], F32, kind="ExternalInput").ap()
    bv_in = nc.dram_tensor("bv", [128, 8], F32, kind="ExternalInput").ap()
    bo_in = nc.dram_tensor("bo", [128, 8], F32, kind="ExternalInput").ap()

    xn0_out = nc.dram_tensor("xn0_out", [C, qext], F32, kind="ExternalOutput").ap()
    a0_out = nc.dram_tensor("a0_out", [C, qtot], F32, kind="ExternalOutput").ap()

    kscr = nc.dram_tensor("kscr", [C, Tk], BF16, kind="Internal").ap()
    vscr = nc.dram_tensor("vscr", [Tk, H0 * SLOT], BF16, kind="Internal").ap()

    with contextlib.ExitStack() as ctx:
        E = ctx.enter_context
        p = P(nc, ctx)
        cst = emit_consts(p, E, nc)

        def phase_barrier():
            for e in ENGINES:
                p.barrier(e)

        wk_s = E(nc.sbuf_tensor("wk_s", [128, 8 * C], BF16))
        wv_s = E(nc.sbuf_tensor("wv_s", [128, 8 * C], BF16))
        wqo_s = E(nc.sbuf_tensor("wqo_s", [128, 8 * C], BF16))
        r_wk, r_wv, r_wqo = _r("wk", "wv", "wqo")
        bias_s = E(nc.sbuf_tensor("bias_s", [128, 32], F32))
        r_bias = Res("bias")
        bqs = E(nc.sbuf_tensor("bqs", [128, 8], F32))
        r_bqs = Res("bqs")
        bv_b = E(nc.sbuf_tensor("bv_b", [128, 8], BF16))
        bprime = E(nc.sbuf_tensor("bprime", [128, 8], F32))
        r_bvb, r_bprime = _r("bvb", "bprime")

        def wload(w_in, w_sb, rr):
            p.op("gpsimd", lambda h: h.dma_start(
                out=w_sb[:].rearrange("p (c n) -> p c n", c=8),
                in_=w_in.rearrange("(c p) n -> p c n", p=128)), writes=[rr])
        wload(wq_in, wqo_s, r_wqo)
        wload(wk_in, wk_s, r_wk)
        wload(wv_in, wv_s, r_wv)
        for i, b_in in enumerate((bq_in, bk_in, bv_in, bo_in)):
            p.op("sync", lambda h, b_in=b_in, i=i: h.dma_start(
                out=bias_s[:, 8 * i:8 * i + 8], in_=b_in), writes=[r_bias])

        def bias_col(i, oc):
            return bias_s[:, 8 * i + oc:8 * i + oc + 1]

        p.op("vector", lambda h: h.tensor_scalar_mul(bqs[:], bias_s[:, 0:8], 0.125),
             reads=[r_bias], writes=[r_bqs])

        xcb = [E(nc.sbuf_tensor(f"xcb{i}", [128, 8 * kch], BF16)) for i in range(2)]
        xnb = [E(nc.sbuf_tensor(f"xnb{i}", [128, 8 * kch], BF16)) for i in range(2)]
        r_xcb = _r("xcb0", "xcb1")
        r_xnb = _r("xnb0", "xnb1")
        xsq = E(nc.sbuf_tensor("xsq", [128, 8 * kch], BF16))
        r_xsq = Res("xsq")
        rowbuf = E(nc.sbuf_tensor("rowbuf", [1, 3 * kch], F32))
        sb_ln = {"rowbuf": rowbuf, "r_rowbuf": Res("rowbuf")}
        mrow = E(nc.sbuf_tensor("mrow", [128, kch // 128], F32))
        r_mrow = Res("mrow")
        qbuf = E(nc.sbuf_tensor("qbuf", [128, 8 * qtot], BF16))
        r_qbuf = Res("qbuf")
        e0col = E(nc.sbuf_tensor("e0col", [128, 1], BF16))
        p.op("vector", lambda h: h.memset(e0col[:], 0.0), writes=[cst["consts_res"]])
        p.op("vector", lambda h: h.memset(e0col[0:1, :], 1.0), writes=[cst["consts_res"]])

        aoin = E(nc.sbuf_tensor("aoin", [128, 8 * qtot], BF16))
        r_aoin = Res("aoin")
        a0st = [E(nc.sbuf_tensor(f"a0st{i}", [128, 512], F32)) for i in range(2)]
        r_a0st = _r("a0st0", "a0st1")

        r_kscr, r_vscr = _r("kscr", "vscr")

        block = E(nc.Block())

        psA = E(nc.psum_tensor("psA", [128, 1536], F32))
        psB = E(nc.psum_tensor("psB", [128, 1024], F32))
        psC = E(nc.psum_tensor("psC", [128, 1024], F32))
        psD = E(nc.psum_tensor("psD", [128, 512], F32))

        phq = contextlib.ExitStack()
        xcf = phq.enter_context(nc.sbuf_tensor("xcf", [128, 8 * kch], F32))
        xnf = phq.enter_context(nc.sbuf_tensor("xnf", [128, 8 * kch], F32))
        r_xcf, r_xnf = _r("xcf", "xnf")
        if True:
            ps_row = (psB[0:1, 0:kch], Res("ps_row"))
            ps_row2 = (psB[0:1, 512:512 + kch], Res("ps_row2"))
            ps_bc = (psC[:, 0:kch], Res("ps_bc"))
            ps_bc2 = (psC[:, 512:512 + kch], Res("ps_bc2"))
            ps_proj = [(psA[:, 0:512], Res("ps_proj0")),
                       (psA[:, 512:1024], Res("ps_proj1"))]
            ps_m = (psD[:, 0:1], Res("ps_m"))
            psr = {"row": ps_row, "row2": ps_row2, "bc": ps_bc, "bc2": ps_bc2}

            r_xn0out, r_a0out = _r("xn0out", "a0out")

            for qc in range(NQC):
                sl = qc % 2
                p.op("sync", lambda h, qc=qc: h.dma_start(
                    out=xcf[:].rearrange("p (c n) -> p c n", c=8),
                    in_=xq.rearrange("(c p) t -> p c t", p=128)[:, :, qc * kch:(qc + 1) * kch]),
                    reads=[], writes=[r_xcf])
                p.op("gpsimd", lambda h, qc=qc, sl=sl: h.dma_start(
                    out=xcb[sl][:].rearrange("p (c n) -> p c n", c=8),
                    in_=xq.rearrange("(c p) t -> p c t", p=128)[:, :, qc * kch:(qc + 1) * kch]),
                    writes=[r_xcb[sl]])
                emit_ln_chunk(p, cst, kch, xcb[sl], r_xcb[sl], xnb[sl], r_xnb[sl],
                              {"row": ps_row, "row2": ps_row2, "bc": ps_bc, "bc2": ps_bc2},
                              sb_ln, src_f32=xcf, r_src_f32=r_xcf,
                              dst_f32=xnf, r_dst_f32=r_xnf, sq=xsq, r_sq=r_xsq)
                p.op("sync", lambda h, qc=qc: h.dma_start(
                    out=xn0_out.rearrange("(c p) t -> p c t", p=128)[:, :, qc * kch:(qc + 1) * kch],
                    in_=xnf[:].rearrange("p (c n) -> p c n", c=8)),
                    reads=[r_xnf], writes=[r_xn0out])
                qh_off = QH - 0
                if (qc + 1) * kch > QH:
                    lo = max(QH, qc * kch) - qc * kch
                    width = kch - lo
                    qdst0 = qc * kch + lo - QH
                    for oc in range(8):
                        po = oc % 2

                        def mm_q(h, sl=sl, oc=oc, po=po, lo=lo, width=width):
                            last = None
                            for cc in range(8):
                                last = h.matmul(
                                    ps_proj[po][0][:, 0:width], w_lhsT(wqo_s, cc, oc),
                                    xnb[sl][:, cc * kch + lo: (cc + 1) * kch],
                                    start=(cc == 0), stop=(cc == 7))
                            return last
                        p.op("tensor", mm_q, reads=[r_wqo, r_xnb[sl]],
                             writes=[ps_proj[po][1]])
                        p.op("vector", lambda h, oc=oc, po=po, qdst0=qdst0, width=width: h.tensor_scalar(
                            qbuf[:, oc * qtot + qdst0: oc * qtot + qdst0 + width],
                            ps_proj[po][0][:, 0:width], 0.125, bqs[:, oc:oc + 1],
                            OP.mult, OP.add),
                            reads=[ps_proj[po][1], r_bqs], writes=[r_qbuf])

            phase_barrier()
            phq.close()
            phs = contextlib.ExitStack()
            kst = [phs.enter_context(nc.sbuf_tensor(f"kst{i}", [128, 8 * kch], BF16))
                   for i in range(2)]
            r_kst = _r("kst0", "kst1")
            vst = [phs.enter_context(nc.sbuf_tensor(
                f"vst{i}", [128, (kch // 128) * H0 * SLOT], BF16)) for i in range(2)]
            r_vst = _r("vst0", "vst1")
            for kc in range(NKC):
                sl = kc % 2
                p.op("gpsimd", lambda h, kc=kc, sl=sl: h.dma_start(
                    out=xcb[sl][:].rearrange("p (c n) -> p c n", c=8),
                    in_=xb.rearrange("(c p) t -> p c t", p=128)[:, :, kc * kch:(kc + 1) * kch]),
                    writes=[r_xcb[sl]])
                emit_ln_chunk(p, cst, kch, xcb[sl], r_xcb[sl], xnb[sl], r_xnb[sl],
                              {"row": ps_row, "row2": ps_row2, "bc": ps_bc, "bc2": ps_bc2},
                              sb_ln, sq=xsq, r_sq=r_xsq)
                for oc in range(8):
                    po = oc % 2

                    def mm_k(h, sl=sl, oc=oc, po=po):
                        last = None
                        for cc in range(8):
                            last = h.matmul(ps_proj[po][0][:, 0:kch], w_lhsT(wk_s, cc, oc),
                                            xnb[sl][:, cc * kch:(cc + 1) * kch],
                                            start=(cc == 0), stop=(cc == 7))
                        return last
                    p.op("tensor", mm_k, reads=[r_wk, r_xnb[sl]], writes=[ps_proj[po][1]])
                    p.op("scalar", lambda h, sl=sl, oc=oc, po=po: h.add(
                        kst[sl][:, oc * kch:(oc + 1) * kch], ps_proj[po][0][:, 0:kch],
                        bias_col(1, oc)),
                        reads=[ps_proj[po][1], r_bias], writes=[r_kst[sl]])
                p.op("sync", lambda h, kc=kc, sl=sl: h.dma_start(
                    out=kscr.rearrange("(c p) t -> p c t", p=128)[:, :, kc * kch:(kc + 1) * kch],
                    in_=kst[sl][:].rearrange("p (c n) -> p c n", c=8)),
                    reads=[r_kst[sl]], writes=[r_kscr])
                for ts in range(kch // 128):
                    p.op("tensor", lambda h, sl=sl, ts=ts: h.matmul(
                        ps_m[0][:], xnb[sl][0:128, ts * 128:(ts + 1) * 128], e0col[:],
                        start=True, stop=True),
                        reads=[r_xnb[sl], cst["consts_res"]], writes=[ps_m[1]])
                    p.op("vector", lambda h, ts=ts: h.tensor_single_scalar(
                        mrow[:, ts:ts + 1], ps_m[0][:], 0.0, OP.not_equal),
                        reads=[ps_m[1]], writes=[r_mrow])
                    for vc in range(2):
                        po = vc % 2

                        def mm_v(h, sl=sl, ts=ts, vc=vc, po=po):
                            last = None
                            for cc in range(8):
                                last = h.matmul(
                                    ps_proj[po][0][:, 0:512],
                                    xnb[sl][:, cc * kch + ts * 128: cc * kch + ts * 128 + 128],
                                    wv_s[:, cc * C + vc * 512: cc * C + vc * 512 + 512],
                                    start=(cc == 0), stop=(cc == 7))
                            return last
                        p.op("tensor", mm_v, reads=[r_wv, r_xnb[sl]], writes=[ps_proj[po][1]])
                        p.op("vector", lambda h, sl=sl, ts=ts, vc=vc, po=po: h.tensor_scalar(
                            vst[sl][:].rearrange("p (u s e) -> p u s e", u=kch // 128, s=H0)[
                                :, ts, vc * 8:(vc + 1) * 8, 0:64],
                            ps_proj[po][0][:].rearrange("p (s e) -> p s e", s=8),
                            mrow[:, ts:ts + 1], None, OP.mult),
                            reads=[ps_proj[po][1], r_mrow], writes=[r_vst[sl]])
                    p.op("vector", lambda h, sl=sl, ts=ts: h.tensor_copy(
                        vst[sl][:].rearrange("p (u s e) -> p u s e", u=kch // 128, s=H0)[
                            :, ts, :, 64:65],
                        mrow[:, ts:ts + 1].broadcast_to([128, H0, 1])),
                        reads=[r_mrow], writes=[r_vst[sl]])
                    p.op("vector", lambda h, sl=sl, ts=ts: h.memset(
                        vst[sl][:].rearrange("p (u s e) -> p u s e", u=kch // 128, s=H0)[
                            :, ts, :, 65:66], 0.0), writes=[r_vst[sl]])
                p.op("sync", lambda h, kc=kc, sl=sl: h.dma_start(
                    out=vscr.rearrange("(k p) s -> p k s", p=128)[
                        :, kc * (kch // 128):(kc + 1) * (kch // 128), :],
                    in_=vst[sl][:].rearrange("p (u s) -> p u s", u=kch // 128)),
                    reads=[r_vst[sl]], writes=[r_vscr])

            p.op("vector", lambda h: h.tensor_copy(bv_b[:], bias_s[:, 16:24]),
                 reads=[r_bias], writes=[r_bvb])
            wload(wo_in, wqo_s, r_wqo)
            for ic in range(8):
                def mm_bp(h, ic=ic):
                    last = None
                    for ec in range(8):
                        last = h.matmul(ps_proj[0][0][:, 0:1], w_lhsT(wqo_s, ec, ic),
                                        bv_b[:, ec:ec + 1], start=(ec == 0), stop=(ec == 7))
                    return last
                p.op("tensor", mm_bp, reads=[r_wqo, r_bvb], writes=[ps_proj[0][1]])
                p.op("vector", lambda h, ic=ic: h.tensor_tensor(
                    bprime[:, ic:ic + 1], ps_proj[0][0][:, 0:1], bias_s[:, 24 + ic:25 + ic],
                    OP.add), reads=[ps_proj[0][1], r_bias], writes=[r_bprime])

        phase_barrier()
        phs.close()
        phm = contextlib.ExitStack()
        khb = [phm.enter_context(nc.sbuf_tensor(f"khb{i}", [128, KHC], BF16))
               for i in range(2)]
        r_khb = _r("khb0", "khb1")
        vhb = [phm.enter_context(nc.sbuf_tensor(
            f"vhb{i}", [128, (KHC // 128) * SLOT], BF16)) for i in range(2)]
        r_vhb = _r("vhb0", "vhb1")
        ptb = [phm.enter_context(nc.sbuf_tensor(f"ptb{i}", [128, 2 * QOUT], BF16))
               for i in range(2)]
        r_ptb = _r("ptb0", "ptb1")
        zrow = phm.enter_context(nc.sbuf_tensor("zrow", [1, QOUT], F32))
        r_zrow = Res("zrow")
        aotmp = phm.enter_context(nc.sbuf_tensor("aotmp", [64, QOUT], BF16))
        r_aotmp = Res("aotmp")
        if True:
            ps_st = [(psB[:, 0:1024], Res("ps_st0")),
                     (psC[:, 0:1024], Res("ps_st1"))]
            ps_ao = [(psA[0:65, 0:QOUT], Res("ps_ao0")),
                     (psA[0:65, 512:512 + QOUT], Res("ps_ao1"))]
            ps_zb = (psD[0:64, 0:QOUT], Res("ps_zb"))

            nhalf = KHC // 128
            for qo in range(NQO):
                for h0 in range(H0):
                    aop = ps_ao[h0 % 2]
                    for kb in range(NKH):
                        lsl = (h0 * NKH + kb) % 2
                        p.op("sync", lambda h, h0=h0, kb=kb, lsl=lsl: h.dma_start(
                            out=khb[lsl][(h0 % 2) * 64:(h0 % 2) * 64 + 64, :],
                            in_=kscr[h0 * 64:(h0 + 1) * 64, kb * KHC:(kb + 1) * KHC]),
                            reads=[r_kscr], writes=[r_khb[lsl]])
                        p.op("sync", lambda h, h0=h0, kb=kb, lsl=lsl: h.dma_start(
                            out=vhb[lsl][:].rearrange("p (u s) -> p u s", s=SLOT),
                            in_=vscr.rearrange("(k p) s -> p k s", p=128)[
                                :, kb * nhalf:(kb + 1) * nhalf,
                                h0 * SLOT:(h0 + 1) * SLOT]),
                            reads=[r_vscr], writes=[r_vhb[lsl]])
                        for sp in range(nhalf // 2):
                            stp = ps_st[sp % 2]
                            ptp = sp % 2

                            def mm_st(h, h0=h0, qo=qo, kb=kb, lsl=lsl, sp=sp, stp=stp):
                                last = None
                                for j in range(2):
                                    ks = sp * 2 + j
                                    last = h.matmul(
                                        stp[0][:, j * QOUT:(j + 1) * QOUT],
                                        khb[lsl][(h0 % 2) * 64:(h0 % 2) * 64 + 64,
                                                 ks * 128:(ks + 1) * 128],
                                        qbuf[:].rearrange("p (c n) -> p c n", c=8)[
                                            (h0 % 2) * 64:(h0 % 2) * 64 + 64,
                                            h0 // 2, qo * QOUT:(qo + 1) * QOUT],
                                        start=True, stop=True)
                                return last
                            p.op("tensor", mm_st, reads=[r_khb[lsl], r_qbuf],
                                 writes=[stp[1]])
                            p.op("scalar", lambda h, stp=stp, ptp=ptp: h.activation(
                                ptb[ptp][:], stp[0][:], AF.Exp, bias=cst["csts"][:, 0:1]),
                                reads=[stp[1], cst["consts_res"]], writes=[r_ptb[ptp]])

                            lastacc = (kb == NKH - 1 and sp == nhalf // 2 - 1)

                            def mm_pv(h, kb=kb, lsl=lsl, sp=sp, ptp=ptp, aop=aop,
                                      first=(kb == 0 and sp == 0), lastacc=lastacc):
                                last = None
                                for j in range(2):
                                    ks = sp * 2 + j
                                    last = h.matmul(
                                        aop[0][:],
                                        vhb[lsl][:, ks * SLOT:ks * SLOT + 65],
                                        ptb[ptp][:, j * QOUT:(j + 1) * QOUT],
                                        start=(first and j == 0),
                                        stop=(lastacc and j == 1))
                                return last
                            p.op("tensor", mm_pv, reads=[r_vhb[lsl], r_ptb[ptp]],
                                 writes=[aop[1]])
                    p.op("vector", lambda h, aop=aop: h.tensor_copy(
                        zrow[:], aop[0][64:65, :]), reads=[aop[1]], writes=[r_zrow])
                    p.op("vector", lambda h: h.reciprocal(zrow[:], zrow[:]),
                         reads=[r_zrow], writes=[r_zrow])
                    p.op("tensor", lambda h: h.matmul(ps_zb[0][:], cst["ones_f"][0:1, 0:64],
                                                      zrow[:], start=True, stop=True),
                         reads=[r_zrow, cst["consts_res"]], writes=[ps_zb[1]])
                    p.op("vector", lambda h, aop=aop: h.tensor_copy(
                        aotmp[:], aop[0][0:64, :]), reads=[aop[1]], writes=[r_aotmp])
                    p.op("vector", lambda h, h0=h0, qo=qo: h.tensor_tensor(
                        aoin[:].rearrange("p (c n) -> p c n", c=8)[
                            (h0 % 2) * 64:(h0 % 2) * 64 + 64, h0 // 2,
                            qo * QOUT:(qo + 1) * QOUT],
                        aotmp[:], ps_zb[0][:], OP.mult),
                        reads=[r_aotmp, ps_zb[1]], writes=[r_aoin])

            for tc in range(qtot // 512):
                for oc in range(8):
                    po = oc % 2
                    psl = ps_st[po]

                    def mm_o(h, tc=tc, oc=oc, psl=psl):
                        last = None
                        for cc in range(8):
                            last = h.matmul(
                                psl[0][:, 0:512], w_lhsT(wqo_s, cc, oc),
                                aoin[:].rearrange("p (c n) -> p c n", c=8)[
                                    :, cc, tc * 512:(tc + 1) * 512],
                                start=(cc == 0), stop=(cc == 7))
                        return last
                    p.op("tensor", mm_o, reads=[r_wqo, r_aoin], writes=[psl[1]])
                    p.op("scalar", lambda h, oc=oc, po=po, psl=psl: h.add(
                        a0st[po][:], psl[0][:, 0:512], bprime[:, oc:oc + 1]),
                        reads=[psl[1], r_bprime], writes=[r_a0st[po]])
                    p.op("sync", lambda h, tc=tc, oc=oc, po=po: h.dma_start(
                        out=a0_out.rearrange("(c p) t -> p c t", p=128)[
                            :, oc, tc * 512:(tc + 1) * 512],
                        in_=a0st[po][:]), reads=[r_a0st[po]], writes=[Res("a0o")])

        phase_barrier()
        phm.close()
        p.final_waits()
        p.finalize(block)
    return nc


def _cmaj(x):
    return np.ascontiguousarray(x.T)


def prep_launch_A_inputs(inputs, core):
    b, q = core // 4, core % 4
    x = inputs["L0_attn_x"]
    xb = _cmaj(x[b])
    lo = q * TLOC - QH
    xq = np.zeros((C, TLOC + QH), np.float32)
    src_lo = max(lo, 0)
    xq[:, src_lo - lo:] = xb[:, src_lo:q * TLOC + TLOC]
    d = dict(
        xb=xb, xq=xq,
        wqT=np.ascontiguousarray(inputs["L0_attn_wq_w"].T),
        wkT=np.ascontiguousarray(inputs["L0_attn_wk_w"].T),
        wvT=np.ascontiguousarray(inputs["L0_attn_wv_w"].T),
        woT=np.ascontiguousarray(inputs["L0_attn_wo_w"].T),
        bq=np.ascontiguousarray(inputs["L0_attn_wq_b"].reshape(8, 128).T),
        bk=np.ascontiguousarray(inputs["L0_attn_wk_b"].reshape(8, 128).T),
        bv=np.ascontiguousarray(inputs["L0_attn_wv_b"].reshape(8, 128).T),
        bo=np.ascontiguousarray(inputs["L0_attn_wo_b"].reshape(8, 128).T),
    )
    return {k: np.ascontiguousarray(v, dtype=np.float32) for k, v in d.items()}




HB = 386


def build_launch_B(Tl=TLOC, n_layers=1, dump=False):
    TBP = Tl + 384
    QW = Tl + 128
    NW = Tl // 256
    YW0 = Tl + HB
    nc = bass.Bass("TRN2", target_bir_lowering=False)
    nc.detect_race_conditions = False

    y_in = nc.dram_tensor("y_in", [C, Tl + HB], F32, kind="ExternalInput").ap()
    ins_l = []
    for l in range(n_layers):
        ins_l.append(dict(
            wqkv=nc.dram_tensor(f"wqkv{l}", [C, 3072], F32, kind="ExternalInput").ap(),
            wo=nc.dram_tensor(f"wo{l}", [1024, 1024], F32, kind="ExternalInput").ap(),
            w1p=nc.dram_tensor(f"w1p{l}", [C, 5632], F32, kind="ExternalInput").ap(),
            w2p=nc.dram_tensor(f"w2p{l}", [2816, 1024], F32, kind="ExternalInput").ap(),
            b1p=nc.dram_tensor(f"b1p{l}", [128, 44], F32, kind="ExternalInput").ap(),
            b2c=nc.dram_tensor(f"b2c{l}", [128, 8], F32, kind="ExternalInput").ap(),
            lng=nc.dram_tensor(f"lng{l}", [128, 8], F32, kind="ExternalInput").ap(),
            lnb=nc.dram_tensor(f"lnb{l}", [128, 8], F32, kind="ExternalInput").ap(),
            qs=nc.dram_tensor(f"qs{l}", [128, 4], F32, kind="ExternalInput").ap(),
        ))
    tabs_in = nc.dram_tensor("tabs", [128, 4 * TBP], F32, kind="ExternalInput").ap()
    selq_in = nc.dram_tensor("selq", [128, 128], F32, kind="ExternalInput").ap()
    selb_in = nc.dram_tensor("selb", [16, 1024], F32, kind="ExternalInput").ap()
    maskw_in = nc.dram_tensor("mask_w", [128, 1024], F32, kind="ExternalInput").ap()
    maskm1_in = nc.dram_tensor("mask_m1", [128, 384], F32, kind="ExternalInput").ap()
    vmask_in = nc.dram_tensor("vmask", [128, TBP // 128], F32, kind="ExternalInput").ap()

    y_out = nc.dram_tensor("y_out", [C, Tl], F32, kind="ExternalOutput").ap()

    if dump:
        qd = nc.dram_tensor("qd", [C, TBP], BF16, kind="ExternalOutput").ap()
        kd = nc.dram_tensor("kd", [C, TBP], BF16, kind="ExternalOutput").ap()
        vd = nc.dram_tensor("vd", [TBP, H0 * SLOT], BF16, kind="ExternalOutput").ap()
        fd = nc.dram_tensor("fd", [QW, 1024], BF16, kind="ExternalOutput").ap()
        y2d = nc.dram_tensor("y2d", [C, Tl], F32, kind="ExternalOutput").ap()
        ptd = nc.dram_tensor("ptd", [128, 16 * 1024], BF16, kind="ExternalOutput").ap()
        aod = nc.dram_tensor("aod", [128, 1536], F32, kind="ExternalOutput").ap()
        vwd = nc.dram_tensor("vwd", [128, 2 * 4 * SLOT], BF16, kind="ExternalOutput").ap()
    qscr = nc.dram_tensor("qscrB", [C, TBP], BF16, kind="Internal").ap()
    kscrB = nc.dram_tensor("kscrB", [C, TBP], BF16, kind="Internal").ap()
    vscrB = nc.dram_tensor("vscrB", [TBP, H0 * SLOT], BF16, kind="Internal").ap()
    flipscr = nc.dram_tensor("flipscr", [QW, 1024], BF16, kind="Internal").ap()

    def slot_off(h):
        return 512 * (h // 7) + 66 * (h % 7)

    with contextlib.ExitStack() as ctx:
        E = ctx.enter_context
        p = P(nc, ctx)
        cst = emit_consts(p, E, nc)

        def phase_barrier():
            for e in ENGINES:
                p.barrier(e)

        ybuf = E(nc.sbuf_tensor("ybuf", [128, 8 * YW0], F32))
        r_y = Res("y")

        def ycol(t):
            return t + HB

        selq_s = E(nc.sbuf_tensor("selq_s", [128, 128], BF16))
        selb_s = E(nc.sbuf_tensor("selb_s", [16, 1024], F32))
        maskw_s = E(nc.sbuf_tensor("maskw_s", [128, 1024], BF16))
        maskm1_s = E(nc.sbuf_tensor("maskm1_s", [128, 384], BF16))
        vmask_s = E(nc.sbuf_tensor("vmask_s", [128, TBP // 128], F32))
        r_sel = Res("sel")
        qscol = E(nc.sbuf_tensor("qscol", [128, 4], F32))
        r_qscol = Res("qscol")
        lgb_s = E(nc.sbuf_tensor("lgb_s", [128, 16], F32))
        b2c_s = E(nc.sbuf_tensor("b2c_s", [128, 8], F32))
        b1p_s = E(nc.sbuf_tensor("b1p_s", [128, 44], F32))
        r_lgb, r_b2c, r_b1p = _r("lgb", "b2c", "b1p")
        r_qscr, r_kscr, r_vscr, r_flip = _r("qscr", "kscr", "vscr", "flip")
        psA = E(nc.psum_tensor("psA", [128, 1536], F32))
        psB = E(nc.psum_tensor("psB", [128, 1024], F32))
        psC = E(nc.psum_tensor("psC", [128, 1024], F32))
        psD = E(nc.psum_tensor("psD", [128, 512], F32))

        block = E(nc.Block())

        p.op("sync", lambda h: h.dma_start(
            out=ybuf[:].rearrange("p (c n) -> p c n", c=8),
            in_=y_in.rearrange("(c p) t -> p c t", p=128)), writes=[r_y])
        p.op("gpsimd", lambda h: h.dma_start(out=selq_s[:], in_=selq_in), writes=[r_sel])
        p.op("gpsimd", lambda h: h.dma_start(out=selb_s[:], in_=selb_in), writes=[r_sel])
        p.op("gpsimd", lambda h: h.dma_start(out=maskw_s[:], in_=maskw_in), writes=[r_sel])
        p.op("gpsimd", lambda h: h.dma_start(out=maskm1_s[:], in_=maskm1_in), writes=[r_sel])
        p.op("sync", lambda h: h.dma_start(out=vmask_s[:], in_=vmask_in), writes=[r_sel])

        NTC = (TBP + 511) // 512

        for l in range(n_layers):
            il = ins_l[l]
            wscope = contextlib.ExitStack()
            wqkv_s = wscope.enter_context(
                nc.sbuf_tensor(f"wqkv_s{l}", [128, 8 * 3072], BF16))
            r_wqkv = Res("wqkv")
            p.op("gpsimd", lambda h, il=il: h.dma_start(
                out=wqkv_s[:].rearrange("p (c n) -> p c n", c=8),
                in_=il["wqkv"].rearrange("(c p) n -> p c n", p=128)), writes=[r_wqkv])
            p.op("sync", lambda h, il=il: h.dma_start(out=qscol[:], in_=il["qs"]),
                 writes=[r_qscol])
            p.op("sync", lambda h, il=il: h.dma_start(out=lgb_s[:, 0:8], in_=il["lng"]),
                 writes=[r_lgb])
            p.op("sync", lambda h, il=il: h.dma_start(out=lgb_s[:, 8:16], in_=il["lnb"]),
                 writes=[r_lgb])
            p.op("sync", lambda h, il=il: h.dma_start(out=b2c_s[:], in_=il["b2c"]),
                 writes=[r_b2c])
            p.op("sync", lambda h, il=il: h.dma_start(out=b1p_s[:], in_=il["b1p"]),
                 writes=[r_b1p])

            with contextlib.ExitStack() as actx:
                AE = actx.enter_context
                xsb = AE(nc.sbuf_tensor("xsb", [128, 8 * 512], BF16))
                qkraw = AE(nc.sbuf_tensor("qkraw", [128, 16 * 512], BF16))

                sqb = AE(nc.sbuf_tensor("sqb", [128, 16 * 512], BF16))
                nrm = AE(nc.sbuf_tensor("nrm", [16, 2 * 512], F32))
                tabc = [AE(nc.sbuf_tensor(f"tabc{i}", [128, 4 * 512], BF16)) for i in range(2)]
                ctab = AE(nc.sbuf_tensor("ctab", [128, 4 * 512], BF16))
                tmpA = AE(nc.sbuf_tensor("tmpA", [128, 512], BF16))
                tmpB = AE(nc.sbuf_tensor("tmpB", [128, 512], BF16))
                rotc = [AE(nc.sbuf_tensor(f"rotc{i}", [128, 512], BF16)) for i in range(2)]
                vstg = AE(nc.sbuf_tensor("vstg", [128, 4 * H0 * SLOT], BF16))
                r_xsb, r_qkraw, r_sqb, r_nrm, r_ctab, r_tmpA, r_tmpB, r_vstg = _r(
                    "xsb", "qkraw", "sqb", "nrm", "ctab", "tmpA", "tmpB", "vstg")
                r_qksw = Res("qksw")
                qkswap = sqb
                r_tabc = _r("tabc0", "tabc1")
                r_rotc = _r("rotc0", "rotc1")
                ps_p = [(psA[:, 0:512], Res("ps_p0")),
                        (psA[:, 512:1024], Res("ps_p1"))]
                ps_n = (psB[0:16, 0:512], Res("ps_n"))
                ps_ib = [(psC[:, 0:512], Res("ps_ib0")),
                         (psC[:, 512:1024], Res("ps_ib1"))]

                for tc in range(NTC):
                    t0 = -384 + tc * 512
                    W = min(512, Tl - t0)
                    for cc in range(8):
                        sh = 1 if cc >= 4 else 0
                        p.op("vector", lambda h, cc=cc, sh=sh, t0=t0, W=W: h.tensor_copy(
                            xsb[:, cc * 512:cc * 512 + W],
                            ybuf[:, cc * YW0 + ycol(t0) - sh: cc * YW0 + ycol(t0) - sh + W]),
                            reads=[r_y], writes=[r_xsb])
                    tsl = tc % 2
                    p.op("gpsimd", lambda h, tc=tc, tsl=tsl, W=W: h.dma_start(
                        out=tabc[tsl][:].rearrange("p (f n) -> p f n", f=4)[:, :, 0:W],
                        in_=tabs_in.rearrange("p (f n) -> p f n", f=4)[
                            :, :, tc * 512:tc * 512 + W]),
                        writes=[r_tabc[tsl]])
                    for f in range(4):
                        p.op("vector", lambda h, f=f, tsl=tsl, W=W: h.tensor_scalar(
                            ctab[:, f * 512:f * 512 + W],
                            tabc[tsl][:, f * 512:f * 512 + W],
                            qscol[:, f:f + 1], None, OP.mult),
                            reads=[r_tabc[tsl], r_qscol], writes=[r_ctab])
                    for oc in range(16):
                        po = oc % 2

                        def mm_qk(h, oc=oc, po=po, W=W):
                            last = None
                            for cc in range(8):
                                last = h.matmul(
                                    ps_p[po][0][:, 0:W],
                                    wqkv_s[:, cc * 3072 + oc * 128: cc * 3072 + oc * 128 + 128],
                                    xsb[:, cc * 512:cc * 512 + W],
                                    start=(cc == 0), stop=(cc == 7))
                            return last
                        p.op("tensor", mm_qk, reads=[r_wqkv, r_xsb], writes=[ps_p[po][1]])
                        p.op("scalar", lambda h, oc=oc, po=po, W=W: h.copy(
                            qkraw[:, oc * 512:oc * 512 + W], ps_p[po][0][:, 0:W]),
                            reads=[ps_p[po][1]], writes=[r_qkraw])
                    for half in range(2):
                        p.op("scalar", lambda h, half=half: h.activation(
                            sqb[:, half * 4096:(half + 1) * 4096],
                            qkraw[:, half * 4096:(half + 1) * 4096], AF.Square),
                            reads=[r_qkraw], writes=[r_sqb])
                    for side in range(2):
                        def mm_n(h, side=side, W=W):
                            last = None
                            for cc in range(8):
                                last = h.matmul(
                                    ps_n[0][:, 0:W], selq_s[:, cc * 16:(cc + 1) * 16],
                                    sqb[:, (side * 8 + cc) * 512: (side * 8 + cc) * 512 + W],
                                    start=(cc == 0), stop=(cc == 7))
                            return last
                        p.op("tensor", mm_n, reads=[r_sel, r_sqb], writes=[ps_n[1]])
                        p.op("scalar", lambda h, side=side, W=W: h.activation(
                            nrm[:, side * 512:side * 512 + W], ps_n[0][:, 0:W], AF.Sqrt),
                            reads=[ps_n[1]], writes=[r_nrm])
                        p.op("vector", lambda h, side=side, W=W: h.tensor_scalar_max(
                            nrm[:, side * 512:side * 512 + W],
                            nrm[:, side * 512:side * 512 + W], 1e-12),
                            reads=[r_nrm], writes=[r_nrm])
                        p.op("vector", lambda h, side=side, W=W: h.reciprocal(
                            nrm[:, side * 512:side * 512 + W],
                            nrm[:, side * 512:side * 512 + W]),
                            reads=[r_nrm], writes=[r_nrm])
                    for sw in range(4):
                        src = (sw ^ 1) * 32
                        p.op("gpsimd", lambda h, sw=sw, src=src: h.dma_start(
                            out=qkswap[sw * 32:sw * 32 + 32, :],
                            in_=qkraw[src:src + 32, :]),
                            reads=[r_qkraw], writes=[r_sqb])
                    for oc in range(16):
                        side = 0 if oc < 8 else 1
                        cc8 = oc % 8
                        pb = oc % 2
                        rsl = oc % 2
                        ct0 = (0 if side == 0 else 2) * 512
                        st0 = (1 if side == 0 else 3) * 512
                        p.op("tensor", lambda h, cc8=cc8, side=side, pb=pb, W=W: h.matmul(
                            ps_ib[pb][0][:, 0:W], selb_s[:, cc8 * 128:(cc8 + 1) * 128],
                            nrm[0:16, side * 512:side * 512 + W], start=True, stop=True),
                            reads=[r_sel, r_nrm], writes=[ps_ib[pb][1]])
                        p.op("vector", lambda h, oc=oc, ct0=ct0, W=W: h.tensor_tensor(
                            tmpA[:, 0:W], qkraw[:, oc * 512:oc * 512 + W],
                            ctab[:, ct0:ct0 + W], OP.mult),
                            reads=[r_qkraw, r_ctab], writes=[r_tmpA])
                        p.op("vector", lambda h, oc=oc, st0=st0, W=W: h.tensor_tensor(
                            tmpB[:, 0:W], qkswap[:, oc * 512:oc * 512 + W],
                            ctab[:, st0:st0 + W], OP.mult),
                            reads=[r_sqb, r_ctab], writes=[r_tmpB])
                        p.op("vector", lambda h, W=W: h.tensor_tensor(
                            tmpA[:, 0:W], tmpA[:, 0:W], tmpB[:, 0:W], OP.add),
                            reads=[r_tmpA, r_tmpB], writes=[r_tmpA])
                        p.op("vector", lambda h, pb=pb, rsl=rsl, W=W: h.tensor_tensor(
                            rotc[rsl][:, 0:W], tmpA[:, 0:W],
                            ps_ib[pb][0][:, 0:W], OP.mult),
                            reads=[r_tmpA, ps_ib[pb][1]], writes=[r_rotc[rsl]])
                        scr = qscr if side == 0 else kscrB
                        rres = r_qscr if side == 0 else r_kscr
                        p.op("sync", lambda h, scr=scr, cc8=cc8, tc=tc, rsl=rsl, W=W: h.dma_start(
                            out=scr.rearrange("(c p) t -> p c t", p=128)[
                                :, cc8, tc * 512:tc * 512 + W],
                            in_=rotc[rsl][:, 0:W]),
                            reads=[r_rotc[rsl]], writes=[rres])
                    for ts in range(W // 128):
                        for vc in range(2):
                            po = vc % 2

                            def mm_v(h, ts=ts, vc=vc, po=po):
                                last = None
                                for cc in range(8):
                                    last = h.matmul(
                                        ps_p[po][0][:, 0:512],
                                        xsb[:, cc * 512 + ts * 128: cc * 512 + ts * 128 + 128],
                                        wqkv_s[:, cc * 3072 + 2048 + vc * 512:
                                               cc * 3072 + 2048 + vc * 512 + 512],
                                        start=(cc == 0), stop=(cc == 7))
                                return last
                            p.op("tensor", mm_v, reads=[r_wqkv, r_xsb], writes=[ps_p[po][1]])
                            p.op("vector", lambda h, tc=tc, ts=ts, vc=vc, po=po: h.tensor_scalar(
                                vstg[:].rearrange("p (u s e) -> p u s e", u=4, s=H0)[
                                    :, ts, vc * 8:(vc + 1) * 8, 0:64],
                                ps_p[po][0][:].rearrange("p (s e) -> p s e", s=8),
                                vmask_s[:, tc * 4 + ts:tc * 4 + ts + 1], None, OP.mult),
                                reads=[ps_p[po][1], r_sel], writes=[r_vstg])
                        p.op("vector", lambda h, tc=tc, ts=ts: h.tensor_copy(
                            vstg[:].rearrange("p (u s e) -> p u s e", u=4, s=H0)[
                                :, ts, :, 64:65],
                            vmask_s[:, tc * 4 + ts:tc * 4 + ts + 1].broadcast_to([128, H0, 1])),
                            reads=[r_sel], writes=[r_vstg])
                        p.op("vector", lambda h, ts=ts: h.memset(
                            vstg[:].rearrange("p (u s e) -> p u s e", u=4, s=H0)[
                                :, ts, :, 65:66], 0.0), writes=[r_vstg])
                    p.op("sync", lambda h, tc=tc, W=W: h.dma_start(
                        out=vscrB.rearrange("(k p) s -> p k s", p=128)[
                            :, tc * 4:tc * 4 + W // 128, :],
                        in_=vstg[:].rearrange("p (u s) -> p u s", u=4)[:, 0:W // 128, :]),
                        reads=[r_vstg], writes=[r_vscr])
                phase_barrier()

            if dump and l == 0:
                p.op("gpsimd", lambda h: h.dma_start(out=qd, in_=qscr),
                     reads=[r_qscr], writes=[Res("qd")])
                p.op("gpsimd", lambda h: h.dma_start(out=kd, in_=kscrB),
                     reads=[r_kscr], writes=[Res("kd")])
                p.op("gpsimd", lambda h: h.dma_start(out=vd, in_=vscrB),
                     reads=[r_vscr], writes=[Res("vd")])
            with contextlib.ExitStack() as actx:
                AE = actx.enter_context
                ptbuf = AE(nc.sbuf_tensor("ptbuf", [128, 16 * 1024], BF16))
                kwb = [AE(nc.sbuf_tensor(f"kwb{i}", [128, 512], BF16)) for i in range(2)]
                qwb = [AE(nc.sbuf_tensor(f"qwb{i}", [128, 256], BF16)) for i in range(2)]
                vwb = [AE(nc.sbuf_tensor(f"vwb{i}", [128, 4 * SLOT], BF16)) for i in range(2)]
                zr2 = AE(nc.sbuf_tensor("zr2", [128, 16], F32))
                aonst = AE(nc.sbuf_tensor("aonst", [128, 1024], BF16))
                r_ptbuf, r_zr2, r_aonst = _r("ptbuf", "zr2", "aonst")
                r_kwb, r_qwb, r_vwb = _r("kwb0", "kwb1"), _r("qwb0", "qwb1"), _r("vwb0", "vwb1")
                ps_st = [(psB[:, 0:1024], Res("ps_stB0")),
                         (psC[:, 0:1024], Res("ps_stB1"))]
                ps_pv = [(psA[:, 0:512], Res("ps_pv0")),
                         (psA[:, 512:1024], Res("ps_pv1"))]

                wlist = [(-1, 128, 3)] + [(w, 256, 4) for w in range(NW)]
                for (w, QBW, nks) in wlist:
                    for h0 in range(H0):
                        lsl = h0 % 2
                        pb = (h0 % 2) * 64
                        if w < 0:
                            kc0, kwd, qc0, qwd = 0, 384, 256, 128
                        else:
                            kc0, kwd = w * 256 + 128, 512
                            qc0, qwd = w * 256 + 384, 256
                        p.op("sync", lambda h, h0=h0, kc0=kc0, kwd=kwd, lsl=lsl, pb=pb: h.dma_start(
                            out=kwb[lsl][pb:pb + 64, 0:kwd],
                            in_=kscrB[h0 * 64:(h0 + 1) * 64, kc0:kc0 + kwd]),
                            reads=[r_kscr], writes=[r_kwb[lsl]])
                        p.op("sync", lambda h, h0=h0, qc0=qc0, qwd=qwd, lsl=lsl, pb=pb: h.dma_start(
                            out=qwb[lsl][pb:pb + 64, 0:qwd],
                            in_=qscr[h0 * 64:(h0 + 1) * 64, qc0:qc0 + qwd]),
                            reads=[r_qscr], writes=[r_qwb[lsl]])
                        stp = ps_st[h0 % 2]

                        def mm_s(h, lsl=lsl, pb=pb, QBW=QBW, nks=nks, qwd=qwd, stp=stp):
                            last = None
                            per_bank = max(1, 512 // QBW)
                            for ks in range(nks):
                                last = h.matmul(
                                    stp[0][:, ks * QBW:ks * QBW + qwd],
                                    kwb[lsl][pb:pb + 64, ks * 128:(ks + 1) * 128],
                                    qwb[lsl][pb:pb + 64, 0:qwd],
                                    start=(ks % per_bank == 0),
                                    stop=(ks % per_bank == per_bank - 1 or ks == nks - 1))
                            return last
                        p.op("tensor", mm_s, reads=[r_kwb[lsl], r_qwb[lsl]], writes=[stp[1]])
                        totw = QBW * nks
                        p.op("scalar", lambda h, h0=h0, totw=totw, stp=stp: h.activation(
                            ptbuf[:, h0 * 1024:h0 * 1024 + totw], stp[0][:, 0:totw],
                            AF.Exp, bias=cst["csts"][:, 0:1], scale=cst["csts"][:, 2:3]),
                            reads=[stp[1], cst["consts_res"]], writes=[r_ptbuf])
                        msk = maskm1_s if w < 0 else maskw_s
                        p.op("vector", lambda h, h0=h0, totw=totw, msk=msk: h.tensor_tensor(
                            ptbuf[:, h0 * 1024:h0 * 1024 + totw],
                            ptbuf[:, h0 * 1024:h0 * 1024 + totw],
                            msk[:, 0:totw], OP.mult),
                            reads=[r_ptbuf, r_sel], writes=[r_ptbuf])
                    if dump and l == 0 and w == 2:
                        p.op("gpsimd", lambda h: h.dma_start(out=ptd, in_=ptbuf[:]),
                             reads=[r_ptbuf], writes=[Res("ptd")])
                    for qs2 in range(2 if QBW == 256 else 1):
                        for h0 in range(H0):
                            lsl = h0 % 2
                            vr0 = 0 if w < 0 else 2 * w + 1
                            pvp = ps_pv[h0 % 2]
                            p.op("sync", lambda h, h0=h0, vr0=vr0, nks=nks, lsl=lsl: h.dma_start(
                                out=vwb[lsl][:].rearrange("p (u s) -> p u s", s=SLOT)[:, 0:nks, :],
                                in_=vscrB.rearrange("(k p) s -> p k s", p=128)[
                                    :, vr0:vr0 + nks, h0 * SLOT:(h0 + 1) * SLOT]),
                                reads=[r_vscr], writes=[r_vwb[lsl]])

                            def mm_pv(h, h0=h0, qs2=qs2, QBW=QBW, nks=nks, lsl=lsl, pvp=pvp):
                                last = None
                                for ks in range(nks):
                                    last = h.matmul(
                                        pvp[0][:, 0:65],
                                        ptbuf[:, h0 * 1024 + ks * QBW + qs2 * 128:
                                              h0 * 1024 + ks * QBW + qs2 * 128 + 128],
                                        vwb[lsl][:, ks * SLOT:ks * SLOT + 65],
                                        start=(ks == 0), stop=(ks == nks - 1))
                                return last
                            p.op("tensor", mm_pv, reads=[r_ptbuf, r_vwb[lsl]],
                                 writes=[pvp[1]])
                            p.op("vector", lambda h, h0=h0, pvp=pvp: h.tensor_scalar(
                                zr2[:, h0:h0 + 1], pvp[0][:, 64:65], 1e-30, None, OP.add),
                                reads=[pvp[1]], writes=[r_zr2])
                            p.op("vector", lambda h, h0=h0: h.reciprocal(
                                zr2[:, h0:h0 + 1], zr2[:, h0:h0 + 1]),
                                reads=[r_zr2], writes=[r_zr2])
                            p.op("vector", lambda h, h0=h0, pvp=pvp: h.tensor_scalar(
                                aonst[:, h0 * 64:(h0 + 1) * 64], pvp[0][:, 0:64],
                                zr2[:, h0:h0 + 1], None, OP.mult),
                                reads=[pvp[1], r_zr2], writes=[r_aonst])
                            if dump and l == 0 and w == 2 and qs2 == 0 and h0 == 1:
                                aoscr = AE(nc.sbuf_tensor("aoscr", [128, 1200], F32))
                                r_aoscr = Res("aoscr")
                                p.op("vector", lambda h: h.tensor_copy(
                                    aoscr[:, 0:1024], psA[:, 0:1024]),
                                    reads=[ps_pv[0][1], ps_pv[1][1]],
                                    writes=[r_aoscr])
                                p.op("vector", lambda h: h.tensor_copy(
                                    aoscr[:, 1024:1040], zr2[:]),
                                    reads=[r_zr2], writes=[r_aoscr])
                                p.op("vector", lambda h: h.tensor_copy(
                                    aoscr[:, 1040:1168], aonst[:, 0:128]),
                                    reads=[r_aonst], writes=[r_aoscr])
                                p.op("gpsimd", lambda h: h.dma_start(
                                    out=aod[:, 0:1168], in_=aoscr[:, 0:1168]),
                                    reads=[r_aoscr], writes=[Res("aod")])
                                for i2 in range(2):
                                    p.op("gpsimd", lambda h, i2=i2: h.dma_start(
                                        out=vwd[:, i2 * 264:(i2 + 1) * 264],
                                        in_=vwb[i2][:]),
                                        reads=[r_vwb[i2]], writes=[Res("vwd")])
                        row0 = (w * 256 + qs2 * 128 + 128) if w >= 0 else 0
                        p.op("sync", lambda h, row0=row0: h.dma_start(
                            out=flipscr[row0:row0 + 128, :], in_=aonst[:]),
                            reads=[r_aonst], writes=[r_flip])
                phase_barrier()

            with contextlib.ExitStack() as actx:
                AE = actx.enter_context
                aoinB = AE(nc.sbuf_tensor("aoinB", [128, 8 * QW], BF16,
                                          is_transpose_dest=True))
                wo_s = AE(nc.sbuf_tensor("wo_sB", [128, 8 * 1024], BF16))
                r_aoinB, r_wo = _r("aoinB", "wo")
                ps_w = [(psB[:, 0:512], Res("ps_w0")),
                        (psC[:, 0:512], Res("ps_w1"))]
                p.op("gpsimd", lambda h, il=il: h.dma_start(
                    out=wo_s[:].rearrange("p (c n) -> p c n", c=8),
                    in_=il["wo"].rearrange("(c p) n -> p c n", p=128)), writes=[r_wo])
                for oc in range(8):
                    p.op("sync", lambda h, oc=oc: h.dma_start_transpose(
                        aoinB[:, oc * QW:(oc + 1) * QW],
                        flipscr[:, oc * 128:(oc + 1) * 128]),
                        reads=[r_flip], writes=[r_aoinB])
                tws = [512] * (QW // 512) + ([QW % 512] if QW % 512 else [])
                tq0 = 0
                for tw in tws:
                    for oc in range(8):
                        po = oc % 2

                        def mm_wo(h, tq0=tq0, tw=tw, oc=oc, po=po):
                            last = None
                            for cc in range(8):
                                last = h.matmul(
                                    ps_w[po][0][:, 0:tw],
                                    wo_s[:, cc * 1024 + oc * 128: cc * 1024 + oc * 128 + 128],
                                    aoinB[:, cc * QW + tq0: cc * QW + tq0 + tw],
                                    start=(cc == 0), stop=(cc == 7))
                            return last
                        p.op("tensor", mm_wo, reads=[r_wo, r_aoinB], writes=[ps_w[po][1]])
                        yc0 = ycol(tq0 - 128)
                        p.op("vector", lambda h, yc0=yc0, tw=tw, oc=oc, po=po: h.tensor_tensor(
                            ybuf[:, oc * YW0 + yc0: oc * YW0 + yc0 + tw],
                            ps_w[po][0][:, 0:tw],
                            ybuf[:, oc * YW0 + yc0: oc * YW0 + yc0 + tw], OP.add),
                            reads=[ps_w[po][1], r_y], writes=[r_y])
                    tq0 += tw
                phase_barrier()

            if dump and l == 0:
                p.op("gpsimd", lambda h: h.dma_start(out=fd, in_=flipscr),
                     reads=[r_flip], writes=[Res("fd")])
                for tcq in range(Tl // 512):
                    p.op("gpsimd", lambda h, tcq=tcq: h.dma_start(
                        out=y2d.rearrange("(c p) t -> p c t", p=128)[
                            :, :, tcq * 512:(tcq + 1) * 512],
                        in_=ybuf[:].rearrange("p (c n) -> p c n", c=8)[
                            :, :, ycol(tcq * 512):ycol(tcq * 512) + 512]),
                        reads=[r_y], writes=[Res("y2d")])
            wscope.close()
            with contextlib.ExitStack() as actx:
                AE = actx.enter_context
                xnb2 = AE(nc.sbuf_tensor("xnb2", [128, 8 * 512], BF16))
                xn2 = AE(nc.sbuf_tensor("xn2", [128, 8 * 512], BF16))
                sq4 = AE(nc.sbuf_tensor("sq4", [128, 8 * 512], BF16))
                wpair = [AE(nc.sbuf_tensor(f"wpair{i}", [128, 8 * 256], BF16)) for i in range(2)]
                w2c = [AE(nc.sbuf_tensor(f"w2c{i}", [128, 22 * 128], BF16)) for i in range(2)]
                geltmp = AE(nc.sbuf_tensor("geltmp", [128, 512], BF16))
                agbuf = AE(nc.sbuf_tensor("agbuf", [128, 22 * 512], BF16))
                rowbuf = AE(nc.sbuf_tensor("rowbufB", [1, 3 * 512], F32))
                sb_ln = {"rowbuf": rowbuf, "r_rowbuf": Res("rowbufB")}
                r_xnb2, r_xn2, r_sq4, r_gel, r_ag = _r("xnb2", "xn2", "sq4", "gel", "ag")
                r_wpair = _r("wpair0", "wpair1")
                r_w2c = _r("w2c0", "w2c1")
                psr = {
                    "row": (psB[0:1, 0:512], Res("ps_rowB")),
                    "row2": (psB[0:1, 512:1024], Res("ps_row2B")),
                    "bc": (psC[:, 0:512], Res("ps_bcB")),
                    "bc2": (psC[:, 512:1024], Res("ps_bc2B")),
                }
                ps_ag = [(psA[:, 0:512], Res("ps_ag0")),
                         (psA[:, 512:1024], Res("ps_ag1"))]
                ps_f = [(psA[:, 1024:1536], Res("ps_f0")),
                        (psD[:, 0:512], Res("ps_f1"))]
                fstg = [AE(nc.sbuf_tensor(f"fstg{i}", [128, 8 * 512], F32))
                        for i in range(2)]
                r_fstg = _r("fstg0", "fstg1")

                def flush_tile(tcf):
                    t0f = tcf * 512
                    slf = tcf % 2
                    for oc in range(8):
                        p.op("vector", lambda h, oc=oc, t0f=t0f, slf=slf: h.tensor_tensor(
                            ybuf[:, oc * YW0 + ycol(t0f): oc * YW0 + ycol(t0f) + 512],
                            ybuf[:, oc * YW0 + ycol(t0f): oc * YW0 + ycol(t0f) + 512],
                            fstg[slf][:, oc * 512:(oc + 1) * 512], OP.add),
                            reads=[r_fstg[slf], r_y], writes=[r_y])

                for tc in range(Tl // 512):
                    t0 = tc * 512
                    for cc in range(8):
                        sh = 1 if cc >= 4 else 0
                        p.op("vector", lambda h, cc=cc, sh=sh, t0=t0: h.tensor_copy(
                            xnb2[:, cc * 512:(cc + 1) * 512],
                            ybuf[:, cc * YW0 + ycol(t0) - sh: cc * YW0 + ycol(t0) - sh + 512]),
                            reads=[r_y], writes=[r_xnb2])

                    def shifted_f32(cc, t0=t0):
                        sh = 1 if cc >= 4 else 0
                        return ybuf[:, cc * YW0 + ycol(t0) - sh: cc * YW0 + ycol(t0) - sh + 512]
                    emit_ln_chunk(p, cst, 512, xnb2, r_xnb2, xn2, r_xn2, psr, sb_ln,
                                  src_f32=shifted_f32, r_src_f32=r_y,
                                  sq=sq4, r_sq=r_sq4,
                                  gb=(lgb_s[:, 0:8], lgb_s[:, 8:16]), r_gb=r_lgb)
                    if tc >= 1:
                        flush_tile(tc - 1)
                    for k in range(22):
                        wsl = k % 2
                        p.op("gpsimd", lambda h, il=il, k=k, wsl=wsl: h.dma_start(
                            out=wpair[wsl][:].rearrange("p (c n) -> p c n", c=8),
                            in_=il["w1p"].rearrange("(c p) n -> p c n", p=128)[
                                :, :, k * 256:(k + 1) * 256]),
                            writes=[r_wpair[wsl]])

                        def mm_a(h, wsl=wsl):
                            last = None
                            for cc in range(8):
                                last = h.matmul(
                                    ps_ag[0][0][:], wpair[wsl][:, cc * 256:cc * 256 + 128],
                                    xn2[:, cc * 512:(cc + 1) * 512],
                                    start=(cc == 0), stop=(cc == 7))
                            return last

                        def mm_g(h, wsl=wsl):
                            last = None
                            for cc in range(8):
                                last = h.matmul(
                                    ps_ag[1][0][:], wpair[wsl][:, cc * 256 + 128:cc * 256 + 256],
                                    xn2[:, cc * 512:(cc + 1) * 512],
                                    start=(cc == 0), stop=(cc == 7))
                            return last
                        p.op("tensor", mm_a, reads=[r_wpair[wsl], r_xn2], writes=[ps_ag[0][1]])
                        p.op("tensor", mm_g, reads=[r_wpair[wsl], r_xn2], writes=[ps_ag[1][1]])
                        p.op("scalar", lambda h, k=k: h.activation(
                            geltmp[:], ps_ag[1][0][:], AF.Gelu,
                            bias=b1p_s[:, 2 * k + 1:2 * k + 2]),
                            reads=[ps_ag[1][1], r_b1p], writes=[r_gel])
                        p.op("vector", lambda h, k=k: h.scalar_tensor_tensor(
                            agbuf[:, k * 512:(k + 1) * 512], ps_ag[0][0][:],
                            b1p_s[:, 2 * k:2 * k + 1], geltmp[:], OP.add, OP.mult),
                            reads=[ps_ag[0][1], r_gel, r_b1p], writes=[r_ag])
                    for oc in range(8):
                        po = oc % 2
                        wsl2 = oc % 2
                        p.op("gpsimd", lambda h, il=il, oc=oc, wsl2=wsl2: h.dma_start(
                            out=w2c[wsl2][:].rearrange("p (k n) -> p k n", k=22),
                            in_=il["w2p"].rearrange("(k p) n -> p k n", p=128)[
                                :, :, oc * 128:(oc + 1) * 128]),
                            writes=[r_w2c[wsl2]])

                        def mm_w2(h, po=po, wsl2=wsl2):
                            last = None
                            for k in range(22):
                                last = h.matmul(
                                    ps_f[po][0][:],
                                    w2c[wsl2][:, k * 128:(k + 1) * 128],
                                    agbuf[:, k * 512:(k + 1) * 512],
                                    start=(k == 0), stop=(k == 21))
                            return last
                        p.op("tensor", mm_w2, reads=[r_w2c[wsl2], r_ag], writes=[ps_f[po][1]])
                        p.op("scalar", lambda h, oc=oc, po=po, tc=tc: h.add(
                            fstg[tc % 2][:, oc * 512:(oc + 1) * 512],
                            ps_f[po][0][:], b2c_s[:, oc:oc + 1]),
                            reads=[ps_f[po][1], r_b2c], writes=[r_fstg[tc % 2]])
                flush_tile(Tl // 512 - 1)
                phase_barrier()

        for tc in range(Tl // 512):
            p.op("sync", lambda h, tc=tc: h.dma_start(
                out=y_out.rearrange("(c p) t -> p c t", p=128)[
                    :, :, tc * 512:(tc + 1) * 512],
                in_=ybuf[:].rearrange("p (c n) -> p c n", c=8)[
                    :, :, ycol(tc * 512):ycol(tc * 512) + 512]),
                reads=[r_y], writes=[Res("yout")])

        p.final_waits()
        p.finalize(block)
    return nc


def make_rotary_tabs(n, t_lo, t_hi):
    d = np.arange(0, 64, 2, dtype=np.float64)
    inv_freq = 1.0 / 10000.0 ** (d / 64.0)
    t = np.arange(n, dtype=np.float64)
    freqs = t[:, None] * inv_freq[None, :]
    freqs = np.concatenate([freqs, freqs], -1)
    base = (d + 0.4 * 64) / (1.4 * 64)
    power = (t - n // 2) / (WIN // 2)
    scale = base[None, :] ** power[:, None]
    scale = np.concatenate([scale, scale], -1)
    cs = np.cos(freqs) * scale
    sn = np.sin(freqs)
    sign = np.where(np.arange(64) < 32, -1.0, 1.0)
    ssp = sn * scale * sign[None, :]
    ci = np.cos(freqs) / scale
    sip = sn / scale * sign[None, :]
    span = t_hi - t_lo
    out = np.zeros((128, 4 * span), np.float32)
    for i, tab in enumerate((cs, ssp, ci, sip)):
        td = tab.T
        tt = np.concatenate([td, td], 0)
        lo = max(t_lo, 0)
        hi = min(t_hi, n)
        if hi > lo:
            out[:, i * span + (lo - t_lo): i * span + (hi - t_lo)] = tt[:, lo:hi]
    return out


def make_selq():
    out = np.zeros((128, 128), np.float32)
    for cc in range(8):
        for pp in range(128):
            out[pp, cc * 16 + 2 * cc + pp // 64] = 1.0
    return out


def make_selb():
    out = np.zeros((16, 1024), np.float32)
    for cc in range(8):
        for pp in range(128):
            out[2 * cc + pp // 64, cc * 128 + pp] = 1.0
    return out


def make_masks_B():
    kk = np.arange(128)[:, None]
    qq = np.arange(256)[None, :]
    m = np.zeros((128, 1024), np.float32)
    m[:, 0:256] = (kk >= qq)
    m[:, 256:512] = (qq - kk <= 128)
    m[:, 512:768] = (kk <= qq)
    m[:, 768:1024] = (qq >= 128 + kk)
    qq1 = np.arange(128)[None, :]
    m1 = np.zeros((128, 384), np.float32)
    m1[:, 0:128] = (kk >= qq1)
    m1[:, 128:256] = 1.0
    m1[:, 256:384] = (kk <= qq1)
    return m, m1


def tile_scale(v):
    t = np.tile(v, 2).astype(np.float32).reshape(128, 1)
    vs = np.concatenate([v[32:64], v[0:32]])
    ts = np.tile(vs, 2).astype(np.float32).reshape(128, 1)
    return t, ts


def prep_B_layer(Wqkv, Wo, W1, b1, W2, b2, g, bln, qsc, ksc):
    W1p = np.zeros((C, 5632), np.float32)
    b1p = np.zeros((128, 44), np.float32)
    W2p = np.zeros((2816, 1024), np.float32)
    for k in range(22):
        wa = W1[:, 128 * k: min(128 * (k + 1), 2730)]
        wg = W1[:, 2730 + 128 * k: min(2730 + 128 * (k + 1), 5460)]
        W1p[:, 256 * k: 256 * k + wa.shape[1]] = wa
        W1p[:, 256 * k + 128: 256 * k + 128 + wg.shape[1]] = wg
        ba = b1[128 * k: min(128 * (k + 1), 2730)]
        bg = b1[2730 + 128 * k: min(2730 + 128 * (k + 1), 5460)]
        b1p[0:len(ba), 2 * k] = ba
        b1p[0:len(bg), 2 * k + 1] = bg
        w2 = W2[128 * k: min(128 * (k + 1), 2730), :]
        W2p[128 * k: 128 * k + w2.shape[0], :] = w2
    qt, qts = tile_scale(qsc)
    kt, kts = tile_scale(ksc)
    qs4 = np.concatenate([qt, qts, kt, kts], 1)
    d = dict(
        wqkv=Wqkv, wo=Wo, w1p=W1p, w2p=W2p, b1p=b1p,
        b2c=b2.reshape(8, 128).T, lng=g.reshape(8, 128).T,
        lnb=bln.reshape(8, 128).T, qs=qs4)
    return {k: np.ascontiguousarray(v, np.float32) for k, v in d.items()}


H1C = 128
BD = 16
DELTA = 64


def build_launch_C(Tl=TLOC):
    TX = Tl + H1C
    NT64 = Tl // 64
    NT128 = Tl // 128
    BI = 16 * 8320
    nc = bass.Bass("TRN2", target_bir_lowering=False)
    nc.detect_race_conditions = False

    x0s = nc.dram_tensor("x0s", [C, TX], F32, kind="ExternalInput").ap()
    a0s = nc.dram_tensor("a0s", [C, TX], F32, kind="ExternalInput").ap()
    f0s = nc.dram_tensor("f0s", [C, TX], F32, kind="ExternalInput").ap()
    wdisp = nc.dram_tensor("wdisp", [C, 256], F32, kind="ExternalInput").ap()
    wval = nc.dram_tensor("wval", [C, 1024], F32, kind="ExternalInput").ap()
    wstr_bd = nc.dram_tensor("wstr_bd", [128, 128], F32, kind="ExternalInput").ap()
    wdmg_bd = nc.dram_tensor("wdmg_bd", [128, 128], F32, kind="ExternalInput").ap()
    wbond_c = nc.dram_tensor("wbond_c", [128, 32], F32, kind="ExternalInput").ap()
    wdo_c = nc.dram_tensor("wdo_c", [128, 32], F32, kind="ExternalInput").ap()
    relT = nc.dram_tensor("relT", [16, 64], F32, kind="ExternalInput").ap()
    wpos = nc.dram_tensor("wpos", [16, 16], F32, kind="ExternalInput").ap()
    selrp = nc.dram_tensor("selrp", [16, 128], F32, kind="ExternalInput").ap()
    bdmg_t = nc.dram_tensor("bdmg_t", [128, 1], F32, kind="ExternalInput").ap()
    bdo_t = nc.dram_tensor("bdo_t", [128, 1], F32, kind="ExternalInput").ap()
    wcproj = nc.dram_tensor("wcproj", [C, 1024], F32, kind="ExternalInput").ap()
    wrT = nc.dram_tensor("wrT", [C, 1024], F32, kind="ExternalInput").ap()
    wkT1 = nc.dram_tensor("wkT1", [C, 1024], F32, kind="ExternalInput").ap()
    wvT1 = nc.dram_tensor("wvT1", [C, 1024], F32, kind="ExternalInput").ap()
    mucols = nc.dram_tensor("mucols", [128, 16], F32, kind="ExternalInput").ap()
    cmask0 = nc.dram_tensor("cmask0", [128, 64], F32, kind="ExternalInput").ap()
    bandm = nc.dram_tensor("bandm", [128, 64], F32, kind="ExternalInput").ap()
    ident_in = nc.dram_tensor("ident128", [128, 128], F32, kind="ExternalInput").ap()

    out_t = nc.dram_tensor("out", [C, Tl], F32, kind="ExternalOutput").ap()

    vscr1 = nc.dram_tensor("vscr1", [TX, 1024], BF16, kind="Internal").ap()
    BSZ = NT64 * BI + 256
    bskew = nc.dram_tensor("bskew", [BSZ], BF16, kind="Internal").ap()

    with contextlib.ExitStack() as ctx:
        E = ctx.enter_context
        p = P(nc, ctx)
        cst = emit_consts(p, E, nc)

        def phase_barrier():
            for e in ENGINES:
                p.barrier(e)

        x1b = E(nc.sbuf_tensor("x1b", [128, 8 * TX], F32))
        xn1 = E(nc.sbuf_tensor("xn1", [128, 8 * TX], BF16))
        r_x1, r_xn1 = _r("x1", "xn1")
        smalls = E(nc.sbuf_tensor("smallsC", [128, 256], F32))
        r_small = Res("small")
        bandm_s = E(nc.sbuf_tensor("bandm_s", [128, 64], BF16))
        wbond_b = E(nc.sbuf_tensor("wbond_b", [128, 32], BF16))
        wdo_b = E(nc.sbuf_tensor("wdo_b", [128, 32], BF16))
        r_wbb = Res("wbb")
        r_vscr1, r_bskew = _r("vscr1", "bskew")
        identb = E(nc.sbuf_tensor("identb", [128, 128], BF16))
        wbdiag = E(nc.sbuf_tensor("wbdiag", [128, 256], BF16))
        r_wbd = Res("wbd")
        r_ident = Res("ident")
        psA = E(nc.psum_tensor("psA", [128, 1536], F32))
        psB = E(nc.psum_tensor("psB", [128, 1024], F32))
        psC = E(nc.psum_tensor("psCc", [128, 512], F32))
        psE = E(nc.psum_tensor("psE", [128, 2048], BF16))

        block = E(nc.Block())

        p.op("sync", lambda h: h.dma_start(out=smalls[:, 64:65], in_=bdmg_t), writes=[r_small])
        p.op("sync", lambda h: h.dma_start(out=smalls[:, 65:66], in_=bdo_t), writes=[r_small])
        p.op("sync", lambda h: h.dma_start(out=smalls[:, 66:82], in_=mucols), writes=[r_small])
        p.op("sync", lambda h: h.dma_start(out=smalls[:, 98:162], in_=cmask0), writes=[r_small])
        p.op("gpsimd", lambda h: h.dma_start(out=bandm_s[:], in_=bandm), writes=[r_wbb])
        p.op("gpsimd", lambda h: h.dma_start(out=wbond_b[:], in_=wbond_c), writes=[r_wbb])
        p.op("gpsimd", lambda h: h.dma_start(out=wdo_b[:], in_=wdo_c), writes=[r_wbb])
        p.op("gpsimd", lambda h: h.dma_start(out=identb[:], in_=ident_in), writes=[r_ident])
        p.op("vector", lambda h: h.tensor_scalar(
            smalls[:, 82:98], smalls[:, 66:82], -1.0, 1.0, OP.mult, OP.add),
            reads=[r_small], writes=[r_small])
        with contextlib.ExitStack() as actx:
            AE = actx.enter_context
            zeros = AE(nc.sbuf_tensor("zerosC", [128, 2048], BF16))
            r_z = Res("z")
            p.op("vector", lambda h: h.memset(zeros[:], 0.0), writes=[r_z])
            nfull = BSZ // 262144
            for k in range(nfull + 1):
                n0 = k * 262144
                n1 = min(BSZ, n0 + 262144)
                if n1 <= n0:
                    break
                cols = (n1 - n0) // 128
                p.op("sync", lambda h, n0=n0, n1=n1, cols=cols: h.dma_start(
                    out=bskew[n0:n1], in_=zeros[:, 0:cols]),
                    reads=[r_z], writes=[r_bskew])

        with contextlib.ExitStack() as actx:
            AE = actx.enter_context
            tmpf = AE(nc.sbuf_tensor("tmpfC", [128, 8 * 512], F32))
            sqc = AE(nc.sbuf_tensor("sqcC", [128, 8 * 512], BF16))
            xbfc = AE(nc.sbuf_tensor("xbfc", [128, 8 * 512], BF16))
            rowb = AE(nc.sbuf_tensor("rowbC", [1, 3 * 512], F32))
            sb_ln = {"rowbuf": rowb, "r_rowbuf": Res("rowbC")}
            r_tmpf, r_sqc, r_xbfc = _r("tmpf", "sqc", "xbfc")
            psr = {
                "row": (psA[0:1, 0:512], Res("ps_rowC")),
                "row2": (psA[0:1, 512:1024], Res("ps_row2C")),
                "bc": (psB[:, 0:512], Res("ps_bcC")),
                "bc2": (psB[:, 512:1024], Res("ps_bc2C")),
            }
            widths = [512] * (TX // 512) + ([TX % 512] if TX % 512 else [])
            c0 = 0
            for W in widths:
                for cc in range(8):
                    p.op("sync", lambda h, cc=cc, c0=c0, W=W: h.dma_start(
                        out=x1b[:, cc * TX + c0: cc * TX + c0 + W],
                        in_=x0s.rearrange("(c p) t -> p c t", p=128)[:, cc, c0:c0 + W]),
                        writes=[r_x1])
                for src in (a0s, f0s):
                    for cc in range(8):
                        p.op("sync", lambda h, src=src, cc=cc, c0=c0, W=W: h.dma_start(
                            out=tmpf[:, cc * 512:cc * 512 + W],
                            in_=src.rearrange("(c p) t -> p c t", p=128)[:, cc, c0:c0 + W]),
                            writes=[r_tmpf])
                    for cc in range(8):
                        p.op("vector", lambda h, cc=cc, c0=c0, W=W: h.tensor_tensor(
                            x1b[:, cc * TX + c0: cc * TX + c0 + W],
                            x1b[:, cc * TX + c0: cc * TX + c0 + W],
                            tmpf[:, cc * 512:cc * 512 + W], OP.add),
                            reads=[r_tmpf, r_x1], writes=[r_x1])

                def xf32(cc, c0=c0, W=W):
                    return x1b[:, cc * TX + c0: cc * TX + c0 + W]

                def xbf(cc, W=W):
                    return xbfc[:, cc * 512:cc * 512 + W]

                def xnd(cc, c0=c0, W=W):
                    return xn1[:, cc * TX + c0: cc * TX + c0 + W]
                for cc in range(8):
                    p.op("vector", lambda h, cc=cc, c0=c0, W=W: h.tensor_copy(
                        xbfc[:, cc * 512:cc * 512 + W],
                        x1b[:, cc * TX + c0: cc * TX + c0 + W]),
                        reads=[r_x1], writes=[r_xbfc])
                emit_ln_chunk(p, cst, W, xbf, r_xbfc, xnd, r_xn1, psr, sb_ln,
                              src_f32=xf32, r_src_f32=r_x1, sq=sqc, r_sq=r_sqc)
                c0 += W
            phase_barrier()

        l1ctx = contextlib.ExitStack()
        dispraw = l1ctx.enter_context(nc.sbuf_tensor("dispraw", [128, 2 * TX], BF16))
        r_disp = Res("disp")
        ltau = l1ctx.enter_context(nc.sbuf_tensor("ltau", [128, (Tl // 128) * 1024], BF16))
        r_ltau = Res("ltau")
        with contextlib.ExitStack() as actx:
            AE = actx.enter_context
            wdv = AE(nc.sbuf_tensor("wdv", [128, 8 * 1280], BF16))
            r_wdv = Res("wdv")
            relw = AE(nc.sbuf_tensor("relw", [16, 80], F32))
            relw_b = AE(nc.sbuf_tensor("relw_b", [16, 80], BF16))
            selrp_s = AE(nc.sbuf_tensor("selrp_s", [16, 128], F32))
            r_relw, r_relwb, r_selrp = _r("relw", "relwb", "selrp")
            vstg1 = AE(nc.sbuf_tensor("vstg1", [128, 1024], BF16))
            r_vstg1 = Res("vstg1")
            rpT_s = AE(nc.sbuf_tensor("rpT_s", [16, 64], F32))
            r_rpT = Res("rpT")
            ps_dv = [(psA[:, 0:512], Res("ps_dv0")),
                     (psA[:, 512:1024], Res("ps_dv1"))]
            r_psC = Res("psCshared")
            ps_rp = (psC[0:16, 0:64], r_psC)
            ps_rpb = (psC[:, 64:128], r_psC)

            p.op("gpsimd", lambda h: h.dma_start(
                out=wdv[:].rearrange("p (c n) -> p c n", c=8)[:, :, 0:256],
                in_=wdisp.rearrange("(c p) n -> p c n", p=128)), writes=[r_wdv])
            p.op("gpsimd", lambda h: h.dma_start(
                out=wdv[:].rearrange("p (c n) -> p c n", c=8)[:, :, 256:1280],
                in_=wval.rearrange("(c p) n -> p c n", p=128)), writes=[r_wdv])
            p.op("gpsimd", lambda h: h.dma_start(
                out=wbdiag[:, 0:128], in_=wstr_bd), writes=[r_wbd])
            p.op("gpsimd", lambda h: h.dma_start(
                out=wbdiag[:, 128:256], in_=wdmg_bd), writes=[r_wbd])
            p.op("sync", lambda h: h.dma_start(out=relw[:, 0:64], in_=relT), writes=[r_relw])
            p.op("sync", lambda h: h.dma_start(out=relw[:, 64:80], in_=wpos), writes=[r_relw])
            p.op("sync", lambda h: h.dma_start(out=selrp_s[:], in_=selrp), writes=[r_selrp])

            p.op("vector", lambda h: h.tensor_copy(relw_b[:], relw[:]),
                 reads=[r_relw], writes=[r_relwb])
            p.op("tensor", lambda h: h.matmul(ps_rp[0][:], relw_b[:, 64:80],
                                              relw_b[:, 0:64], start=True, stop=True),
                 reads=[r_relwb], writes=[ps_rp[1]])
            p.op("vector", lambda h: h.tensor_copy(rpT_s[:], ps_rp[0][:]),
                 reads=[ps_rp[1]], writes=[r_rpT])
            p.op("tensor", lambda h: h.matmul(ps_rpb[0][:], selrp_s[:], rpT_s[:],
                                              start=True, stop=True),
                 reads=[r_selrp, r_rpT], writes=[ps_rpb[1]])
            p.op("vector", lambda h: h.tensor_copy(smalls[:, 0:64], ps_rpb[0][:]),
                 reads=[ps_rpb[1]], writes=[r_small])

            cw = [512] * (TX // 512) + ([TX % 512] if TX % 512 else [])
            c0 = 0
            for W in cw:
                for oc in range(2):
                    po = oc % 2

                    def mm_d(h, oc=oc, po=po, c0=c0, W=W):
                        last = None
                        for cc in range(8):
                            last = h.matmul(
                                ps_dv[po][0][:, 0:W],
                                wdv[:, cc * 1280 + oc * 128: cc * 1280 + oc * 128 + 128],
                                xn1[:, cc * TX + c0: cc * TX + c0 + W],
                                start=(cc == 0), stop=(cc == 7))
                        return last
                    p.op("tensor", mm_d, reads=[r_wdv, r_xn1], writes=[ps_dv[po][1]])
                    p.op("scalar", lambda h, oc=oc, po=po, c0=c0, W=W: h.copy(
                        dispraw[:, oc * TX + c0: oc * TX + c0 + W],
                        ps_dv[po][0][:, 0:W]),
                        reads=[ps_dv[po][1]], writes=[r_disp])
                for ts in range(W // 128):
                    for vc in range(2):
                        po = vc % 2

                        def mm_v(h, ts=ts, vc=vc, po=po, c0=c0):
                            last = None
                            for cc in range(8):
                                last = h.matmul(
                                    ps_dv[po][0][:, 0:512],
                                    xn1[:, cc * TX + c0 + ts * 128: cc * TX + c0 + ts * 128 + 128],
                                    wdv[:, cc * 1280 + 256 + vc * 512: cc * 1280 + 256 + vc * 512 + 512],
                                    start=(cc == 0), stop=(cc == 7))
                            return last
                        p.op("tensor", mm_v, reads=[r_wdv, r_xn1], writes=[ps_dv[po][1]])
                        p.op("vector", lambda h, vc=vc, po=po: h.tensor_copy(
                            vstg1[:, vc * 512:(vc + 1) * 512], ps_dv[po][0][:, 0:512]),
                            reads=[ps_dv[po][1]], writes=[r_vstg1])
                    p.op("sync", lambda h, c0=c0, ts=ts: h.dma_start(
                        out=vscr1.rearrange("(k p) s -> p k s", p=128)[:, c0 // 128 + ts, :],
                        in_=vstg1[:]), reads=[r_vstg1], writes=[r_vscr1])
                c0 += W
            phase_barrier()

        with contextlib.ExitStack() as actx:
            AE = actx.enter_context
            ppqq = AE(nc.sbuf_tensor("ppqq", [128, 4 * TX], F32))
            r_ppqq = Res("ppqq")
            plb = AE(nc.sbuf_tensor("plb", [128, 2 * 512], F32))
            pld = AE(nc.sbuf_tensor("pld", [128, 2 * 512], F32))
            gelb = AE(nc.sbuf_tensor("gelb", [128, 2 * 512], BF16))
            geld = AE(nc.sbuf_tensor("geld", [128, 2 * 512], BF16))
            dmgs = AE(nc.sbuf_tensor("dmgs", [128, 512], F32))
            lst = AE(nc.sbuf_tensor("lst", [128, 512], BF16))
            r_plb, r_pld, r_gelb, r_geld, r_dmgs, r_lst = _r(
                "plb", "pld", "gelb", "geld", "dmgs", "lst")
            ps_dv = [(psA[:, 0:512], Res("ps_dvb0")),
                     (psA[:, 512:1024], Res("ps_dvb1"))]
            ps_lb = (psB[:, 0:512], Res("ps_lb"))
            ps_ld = (psB[:, 512:1024], Res("ps_ld"))
            ps_tr = [(psE[:, 0:128], Res("ps_tr0")),
                     (psE[:, 1024:1152], Res("ps_tr1"))]
            cw = [512] * (TX // 512) + ([TX % 512] if TX % 512 else [])
            c0 = 0
            for W in cw:
                for oc in range(2):
                    for tgt in range(2):
                        po = tgt % 2

                        def mm_pq(h, oc=oc, tgt=tgt, po=po, c0=c0, W=W):
                            return h.matmul(
                                ps_dv[po][0][:, 0:W], wbdiag[:, tgt * 128:tgt * 128 + 128],
                                dispraw[:, oc * TX + c0: oc * TX + c0 + W],
                                start=True, stop=True)
                        p.op("tensor", mm_pq, reads=[r_wbd, r_disp], writes=[ps_dv[po][1]])
                        p.op("scalar", lambda h, oc=oc, tgt=tgt, po=po, c0=c0, W=W: h.copy(
                            ppqq[:, (2 * tgt + oc) * TX + c0: (2 * tgt + oc) * TX + c0 + W],
                            ps_dv[po][0][:, 0:W]),
                            reads=[ps_dv[po][1]], writes=[r_ppqq])
                c0 += W

            for tch in range(Tl // 512):
                t0 = tch * 512
                for dg in range(16):
                    for d4 in range(4):
                        d = dg * 4 + d4
                        off = H1C - 63 + d
                        for ch in range(2):
                            p.op("vector", lambda h, ch=ch, t0=t0, off=off: h.tensor_tensor(
                                plb[:, ch * 512:(ch + 1) * 512],
                                ppqq[:, ch * TX + t0 + off: ch * TX + t0 + off + 512],
                                ppqq[:, ch * TX + t0 + H1C: ch * TX + t0 + H1C + 512],
                                OP.subtract),
                                reads=[r_ppqq], writes=[r_plb])
                            p.op("vector", lambda h, ch=ch, t0=t0, off=off: h.tensor_tensor(
                                pld[:, ch * 512:(ch + 1) * 512],
                                ppqq[:, (2 + ch) * TX + t0 + off: (2 + ch) * TX + t0 + off + 512],
                                ppqq[:, (2 + ch) * TX + t0 + H1C: (2 + ch) * TX + t0 + H1C + 512],
                                OP.subtract),
                                reads=[r_ppqq], writes=[r_pld])
                        p.op("scalar", lambda h, d=d: h.activation(
                            gelb[:], plb[:], AF.Gelu, bias=smalls[:, d:d + 1]),
                            reads=[r_plb, r_small], writes=[r_gelb])
                        p.op("scalar", lambda h: h.activation(
                            geld[:], pld[:], AF.Gelu, bias=smalls[:, 64:65]),
                            reads=[r_pld, r_small], writes=[r_geld])

                        def mm_lb(h, d4=d4):
                            last = None
                            for ch in range(2):
                                last = h.matmul(
                                    ps_lb[0][32 * d4:32 * d4 + 16, :],
                                    wbond_b[:, ch * 16:(ch + 1) * 16],
                                    gelb[:, ch * 512:(ch + 1) * 512],
                                    start=(ch == 0), stop=(ch == 1),
                                    tile_position=(0, 32 * d4))
                            return last
                        p.op("tensor", mm_lb, reads=[r_gelb, r_wbb], writes=[ps_lb[1]])

                        def mm_ld(h, d4=d4):
                            last = None
                            for ch in range(2):
                                last = h.matmul(
                                    ps_ld[0][32 * d4:32 * d4 + 16, :],
                                    wdo_b[:, ch * 16:(ch + 1) * 16],
                                    geld[:, ch * 512:(ch + 1) * 512],
                                    start=(ch == 0), stop=(ch == 1),
                                    tile_position=(0, 32 * d4))
                            return last
                        p.op("tensor", mm_ld, reads=[r_geld, r_wbb], writes=[ps_ld[1]])
                    p.op("scalar", lambda h: h.activation(
                        dmgs[:], ps_ld[0][:], AF.Sigmoid, bias=smalls[:, 65:66]),
                        reads=[ps_ld[1], r_small], writes=[r_dmgs])
                    p.op("vector", lambda h: h.scalar_tensor_tensor(
                        lst[:], dmgs[:], -10.0, ps_lb[0][:], OP.mult, OP.add),
                        reads=[r_dmgs, ps_lb[1]], writes=[r_lst])
                    for j in range(4):
                        trp = ps_tr[j % 2]
                        tbase = 0 if j % 2 == 0 else 1024
                        p.op("tensor", lambda h, j=j, trp=trp: h.transpose(
                            trp[0][:], lst[:, j * 128:(j + 1) * 128], identb[:]),
                            reads=[r_lst, r_ident], writes=[trp[1]])
                        p.op("vector", lambda h, tch=tch, dg=dg, j=j, trp=trp, tbase=tbase: h.tensor_copy(
                            bass.AP(ltau, (tch * 4 + j) * 1024 + dg * 4,
                                    [[(Tl // 128) * 1024, 128], [1, 4], [64, 16]]),
                            bass.AP(psE, tbase, [[2048, 128], [32, 4], [1, 16]])),
                            reads=[trp[1]], writes=[r_ltau])
            phase_barrier()

        with contextlib.ExitStack() as actx:
            AE = actx.enter_context
            wsm = AE(nc.sbuf_tensor("wsm", [128, NT128 * 1024], BF16))
            zsum = AE(nc.sbuf_tensor("zsum", [128, 16], F32))
            r_wsm, r_zsum = _r("wsm", "zsum")

            for u in range(NT128):
                p.op("scalar", lambda h, u=u: h.activation(
                    wsm[:, u * 1024:(u + 1) * 1024], ltau[:, u * 1024:(u + 1) * 1024],
                    AF.Exp), reads=[r_ltau], writes=[r_wsm])
                if u == 0:
                    p.op("vector", lambda h: h.tensor_tensor(
                        bass.AP(wsm, 0, [[NT128 * 1024, 128], [64, 16], [1, 64]]),
                        bass.AP(wsm, 0, [[NT128 * 1024, 128], [64, 16], [1, 64]]),
                        smalls[:, 98:162].unsqueeze(1).broadcast_to([128, 16, 64]),
                        OP.mult), reads=[r_wsm, r_small], writes=[r_wsm])
                p.op("vector", lambda h, u=u: h.reduce_sum(
                    zsum[:], wsm[:, u * 1024:(u + 1) * 1024].rearrange(
                        "p (s d) -> p s d", d=64),
                    axis=mybir.AxisListType.X),
                    reads=[r_wsm], writes=[r_zsum])
                p.op("vector", lambda h: h.reciprocal(zsum[:], zsum[:]),
                     reads=[r_zsum], writes=[r_zsum])
                p.op("vector", lambda h, u=u: h.tensor_tensor(
                    wsm[:, u * 1024:(u + 1) * 1024].rearrange("p (s d) -> p s d", d=64),
                    wsm[:, u * 1024:(u + 1) * 1024].rearrange("p (s d) -> p s d", d=64),
                    zsum[:].unsqueeze(2).broadcast_to([128, 16, 64]), OP.mult),
                    reads=[r_wsm, r_zsum], writes=[r_wsm])
            for hh in range(H0):
                p.op("sync", lambda h, hh=hh: h.dma_start(
                    out=bass.AP(bskew.tensor, hh * 8320 + 129,
                                [[129, 64], [2 * BI, NT128], [1, 64]]),
                    in_=bass.AP(wsm, hh * 64, [[NT128 * 1024, 64], [1024, NT128], [1, 64]])),
                    reads=[r_wsm], writes=[r_bskew])
                p.op("sync", lambda h, hh=hh: h.dma_start(
                    out=bass.AP(bskew.tensor, BI + hh * 8320 + 129,
                                [[129, 64], [2 * BI, NT128], [1, 64]]),
                    in_=bass.AP(wsm, 64 * NT128 * 1024 + hh * 64,
                                [[NT128 * 1024, 64], [1024, NT128], [1, 64]])),
                    reads=[r_wsm], writes=[r_bskew])
                p.op("sync", lambda h, hh=hh: h.dma_start(
                    out=bass.AP(bskew.tensor, BI + hh * 8320,
                                [[BI, 1], [2 * BI, NT128], [1, 64]]),
                    in_=bass.AP(wsm, 63 * NT128 * 1024 + hh * 64,
                                [[NT128 * 1024, 1], [1024, NT128], [1, 64]])),
                    reads=[r_wsm], writes=[r_bskew])
                p.op("sync", lambda h, hh=hh: h.dma_start(
                    out=bass.AP(bskew.tensor, 2 * BI + hh * 8320,
                                [[BI, 1], [2 * BI, NT128 - 1], [1, 64]]),
                    in_=bass.AP(wsm, 127 * NT128 * 1024 + hh * 64,
                                [[NT128 * 1024, 1], [1024, NT128 - 1], [1, 64]])),
                    reads=[r_wsm], writes=[r_bskew])
            phase_barrier()
        l1ctx.close()

        with contextlib.ExitStack() as actx:
            AE = actx.enter_context
            vw1 = [AE(nc.sbuf_tensor(f"vw1_{i}", [128, 1024], BF16)) for i in range(2)]
            r_vw1 = _r("vw10", "vw11")
            ptl = [AE(nc.sbuf_tensor(f"ptl{i}", [128, 1040], BF16,
                                     is_transpose_dest=True)) for i in range(2)]
            r_ptl = _r("ptl0", "ptl1")
            a1pre = AE(nc.sbuf_tensor("a1pre", [128, 8 * Tl], BF16))
            r_a1pre = Res("a1pre")
            wcp = AE(nc.sbuf_tensor("wcp", [128, 8 * 1024], BF16))
            r_wcp = Res("wcp")
            ps_pv = [(psB[0:64, 0:512], Res("ps_pv0")),
                     (psB[0:64, 512:1024], Res("ps_pv1"))]
            ps_cp = [(psA[:, 0:512], Res("ps_cp0")),
                     (psA[:, 512:1024], Res("ps_cp1"))]
            for i in range(NT64):
                sl = i % 2
                p.op("sync", lambda h, i=i, sl=sl: h.dma_start(
                    out=vw1[sl][:], in_=vscr1[64 * (i + 1):64 * (i + 1) + 128, :]),
                    reads=[r_vscr1], writes=[r_vw1[sl]])
                p.op("sync", lambda h, i=i, sl=sl: h.dma_start_transpose(
                    ptl[sl][:], bskew[i * BI:i * BI + 1040 * 128].rearrange(
                        "(r c) -> r c", c=128)),
                    reads=[r_bskew], writes=[r_ptl[sl]])
                p.op("vector", lambda h, sl=sl: h.tensor_tensor(
                    bass.AP(ptl[sl], 1, [[1040, 128], [65, 16], [1, 64]]),
                    bass.AP(ptl[sl], 1, [[1040, 128], [65, 16], [1, 64]]),
                    bandm_s[:].unsqueeze(1).broadcast_to([128, 16, 64]), OP.mult),
                    reads=[r_ptl[sl], r_wbb], writes=[r_ptl[sl]])
                for half in range(2):
                    pvp = ps_pv[half]
                    for hh in range(8 * half, 8 * half + 8):
                        def mm_pv1(h, i=i, sl=sl, hh=hh, half=half, pvp=pvp):
                            return h.matmul(
                                pvp[0][:, (hh % 8) * 64:(hh % 8) * 64 + 64],
                                vw1[sl][:, hh * 64:(hh + 1) * 64],
                                ptl[sl][:, 65 * hh + 1: 65 * hh + 65],
                                start=(hh % 8 == 0), stop=(hh % 8 == 7))
                        p.op("tensor", mm_pv1, reads=[r_vw1[sl], r_ptl[sl]],
                             writes=[pvp[1]])
                    for hh in range(8 * half, 8 * half + 8):
                        p.op("vector", lambda h, i=i, hh=hh, pvp=pvp: h.tensor_copy(
                            a1pre[:].rearrange("p (c n) -> p c n", c=8)[
                                (hh % 2) * 64:(hh % 2) * 64 + 64, hh // 2,
                                i * 64:(i + 1) * 64],
                            pvp[0][:, (hh % 8) * 64:(hh % 8) * 64 + 64]),
                            reads=[pvp[1]], writes=[r_a1pre])
            p.op("gpsimd", lambda h: h.dma_start(
                out=wcp[:].rearrange("p (c n) -> p c n", c=8),
                in_=wcproj.rearrange("(c p) n -> p c n", p=128)), writes=[r_wcp])
            for tc in range(Tl // 512):
                for oc in range(8):
                    po = oc % 2

                    def mm_cp(h, tc=tc, oc=oc, po=po):
                        last = None
                        for cc in range(8):
                            last = h.matmul(
                                ps_cp[po][0][:],
                                wcp[:, cc * 1024 + oc * 128: cc * 1024 + oc * 128 + 128],
                                a1pre[:].rearrange("p (c n) -> p c n", c=8)[
                                    :, cc, tc * 512:(tc + 1) * 512],
                                start=(cc == 0), stop=(cc == 7))
                        return last
                    p.op("tensor", mm_cp, reads=[r_wcp, r_a1pre], writes=[ps_cp[po][1]])
                    p.op("vector", lambda h, tc=tc, oc=oc, po=po: h.tensor_tensor(
                        x1b[:, oc * TX + H1C + tc * 512: oc * TX + H1C + (tc + 1) * 512],
                        ps_cp[po][0][:],
                        x1b[:, oc * TX + H1C + tc * 512: oc * TX + H1C + (tc + 1) * 512],
                        OP.add),
                        reads=[ps_cp[po][1], r_x1], writes=[r_x1])
            phase_barrier()

        with contextlib.ExitStack() as actx:
            AE = actx.enter_context
            wr_s = AE(nc.sbuf_tensor("wr_s", [128, 8 * 1024], BF16))
            wk_s1 = AE(nc.sbuf_tensor("wk_s1", [128, 8 * 1024], BF16))
            wv_s1 = AE(nc.sbuf_tensor("wv_s1", [128, 8 * 1024], BF16))
            r_wr, r_wk1, r_wv1 = _r("wr", "wk1", "wv1")
            xrb = AE(nc.sbuf_tensor("xrb", [128, 8 * 512], BF16))
            xkb = AE(nc.sbuf_tensor("xkb", [128, 8 * 512], BF16))
            tmpx = AE(nc.sbuf_tensor("tmpx", [128, 512], F32))
            r_xrb, r_xkb, r_tmpx = _r("xrb", "xkb", "tmpx")
            rsig = AE(nc.sbuf_tensor("rsig", [128, 8 * 512], BF16))
            k2b = AE(nc.sbuf_tensor("k2b", [128, 8 * 512], BF16))
            krelu = AE(nc.sbuf_tensor("krelu", [128, 512], BF16))
            r_rsig, r_k2b, r_krelu = _r("rsig", "k2b", "krelu")
            f1t = AE(nc.sbuf_tensor("f1t", [128, 512], BF16))
            r_f1t = Res("f1t")
            outst = [AE(nc.sbuf_tensor(f"outst{i}", [128, 512], F32)) for i in range(2)]
            r_outst = _r("outst0", "outst1")
            ps_m1 = [(psA[:, 0:512], Res("ps_m10")),
                     (psA[:, 512:1024], Res("ps_m11"))]

            for w_in, w_sb, rr in ((wrT, wr_s, r_wr), (wkT1, wk_s1, r_wk1),
                                   (wvT1, wv_s1, r_wv1)):
                p.op("gpsimd", lambda h, w_in=w_in, w_sb=w_sb: h.dma_start(
                    out=w_sb[:].rearrange("p (c n) -> p c n", c=8),
                    in_=w_in.rearrange("(c p) n -> p c n", p=128)), writes=[rr])

            for tc in range(Tl // 512):
                t0 = H1C + tc * 512
                for cc in range(8):
                    for (dst, rdst, mbase) in ((xrb, r_xrb, 66), (xkb, r_xkb, 74)):
                        mcol = mbase + cc
                        p.op("vector", lambda h, cc=cc, t0=t0, mcol=mcol: h.tensor_scalar(
                            tmpx[:], xn1[:, cc * TX + t0 - 1: cc * TX + t0 - 1 + 512],
                            smalls[:, 16 + mcol:17 + mcol], None, OP.mult),
                            reads=[r_xn1, r_small], writes=[r_tmpx])
                        p.op("vector", lambda h, cc=cc, t0=t0, dst=dst, mcol=mcol: h.scalar_tensor_tensor(
                            dst[:, cc * 512:(cc + 1) * 512],
                            xn1[:, cc * TX + t0: cc * TX + t0 + 512],
                            smalls[:, mcol:mcol + 1], tmpx[:], OP.mult, OP.add),
                            reads=[r_xn1, r_small, r_tmpx], writes=[rdst])
                for oc in range(8):
                    po = oc % 2

                    def mm_r(h, oc=oc, po=po):
                        last = None
                        for cc in range(8):
                            last = h.matmul(
                                ps_m1[po][0][:],
                                wr_s[:, cc * 1024 + oc * 128: cc * 1024 + oc * 128 + 128],
                                xrb[:, cc * 512:(cc + 1) * 512],
                                start=(cc == 0), stop=(cc == 7))
                        return last
                    p.op("tensor", mm_r, reads=[r_wr, r_xrb], writes=[ps_m1[po][1]])
                    p.op("scalar", lambda h, oc=oc, po=po: h.activation(
                        rsig[:, oc * 512:(oc + 1) * 512], ps_m1[po][0][:], AF.Sigmoid),
                        reads=[ps_m1[po][1]], writes=[r_rsig])
                for oc in range(8):
                    po = oc % 2

                    def mm_k(h, oc=oc, po=po):
                        last = None
                        for cc in range(8):
                            last = h.matmul(
                                ps_m1[po][0][:],
                                wk_s1[:, cc * 1024 + oc * 128: cc * 1024 + oc * 128 + 128],
                                xkb[:, cc * 512:(cc + 1) * 512],
                                start=(cc == 0), stop=(cc == 7))
                        return last
                    p.op("tensor", mm_k, reads=[r_wk1, r_xkb], writes=[ps_m1[po][1]])
                    p.op("scalar", lambda h, oc=oc, po=po: h.activation(
                        krelu[:], ps_m1[po][0][:], AF.Relu),
                        reads=[ps_m1[po][1]], writes=[r_krelu])
                    p.op("scalar", lambda h, oc=oc: h.activation(
                        k2b[:, oc * 512:(oc + 1) * 512], krelu[:], AF.Square),
                        reads=[r_krelu], writes=[r_k2b])
                for oc in range(8):
                    po = oc % 2

                    def mm_kv(h, oc=oc, po=po):
                        last = None
                        for cc in range(8):
                            last = h.matmul(
                                ps_m1[po][0][:],
                                wv_s1[:, cc * 1024 + oc * 128: cc * 1024 + oc * 128 + 128],
                                k2b[:, cc * 512:(cc + 1) * 512],
                                start=(cc == 0), stop=(cc == 7))
                        return last
                    p.op("tensor", mm_kv, reads=[r_wv1, r_k2b], writes=[ps_m1[po][1]])
                    p.op("vector", lambda h, oc=oc, po=po: h.tensor_tensor(
                        f1t[:], ps_m1[po][0][:], rsig[:, oc * 512:(oc + 1) * 512],
                        OP.mult), reads=[ps_m1[po][1], r_rsig], writes=[r_f1t])
                    p.op("vector", lambda h, oc=oc, tc=tc, po=po: h.tensor_tensor(
                        outst[po][:], f1t[:],
                        x1b[:, oc * TX + H1C + tc * 512: oc * TX + H1C + tc * 512 + 512],
                        OP.add), reads=[r_f1t, r_x1], writes=[r_outst[po]])
                    p.op("sync", lambda h, oc=oc, tc=tc, po=po: h.dma_start(
                        out=out_t.rearrange("(c p) t -> p c t", p=128)[
                            :, oc, tc * 512:(tc + 1) * 512],
                        in_=outst[po][:]), reads=[r_outst[po]], writes=[Res("o")])

        p.final_waits()
        p.finalize(block)
    return nc


def prep_C_consts(core_start):
    kk = np.arange(128)[:, None]
    tt = np.arange(64)[None, :]
    bandm = ((kk > tt) & (kk <= tt + 64)).astype(np.float32)
    pp_ = np.arange(128)[:, None]
    dd_ = np.arange(64)[None, :]
    cmask0 = ((core_start + pp_) - 63 + dd_ >= 0).astype(np.float32)
    selrp = np.zeros((16, 128), np.float32)
    for pp2 in range(128):
        selrp[pp2 % 16, pp2] = 1.0
    return dict(bandm=bandm, cmask0=cmask0, selrp=selrp,
                ident128=np.eye(128, dtype=np.float32))


def prep_C_weights(ins):
    Ws = ins["L1_attn_W_strain"]
    Wd = ins["L1_attn_W_dmg"]
    wb = ins["L1_attn_W_bond"][:, 0]
    wdo = ins["L1_attn_W_dmg_out"][:, 0]
    wbond_c = np.zeros((128, 32), np.float32)
    wdo_c = np.zeros((128, 32), np.float32)
    for ch in range(2):
        for hl in range(8):
            for e in range(16):
                wbond_c[16 * hl + e, ch * 16 + 8 * ch + hl] = wb[e]
                wdo_c[16 * hl + e, ch * 16 + 8 * ch + hl] = wdo[e]
    d = dict(
        wdisp=ins["L1_attn_W_disp"], wval=ins["L1_attn_W_val"],
        wstr_bd=np.kron(np.eye(8), Ws), wdmg_bd=np.kron(np.eye(8), Wd),
        wbond_c=wbond_c, wdo_c=wdo_c,
        relT=np.ascontiguousarray(ins["L1_attn_rel_pos_emb"][:64].T),
        wpos=ins["L1_attn_W_pos"],
        bdmg_t=np.tile(ins["L1_attn_b_dmg"], 8).reshape(128, 1),
        bdo_t=np.full((128, 1), ins["L1_attn_b_dmg_out"][0]),
        wcproj=ins["L1_attn_W_cproj"],
        wrT=np.ascontiguousarray(ins["L1_ffn_W_r"].T),
        wkT1=np.ascontiguousarray(ins["L1_ffn_W_k"].T),
        wvT1=np.ascontiguousarray(ins["L1_ffn_W_v"].T),
        mucols=np.concatenate(
            [ins["L1_ffn_mu_r"].reshape(8, 128).T,
             ins["L1_ffn_mu_k"].reshape(8, 128).T], axis=1),
    )
    return {k: np.ascontiguousarray(v, np.float32) for k, v in d.items()}


def _slice_pad(arr, lo, hi):
    out = np.zeros((arr.shape[0], hi - lo), np.float32)
    a = max(lo, 0)
    b2 = min(hi, arr.shape[1])
    if b2 > a:
        out[:, a - lo:b2 - lo] = arr[:, a:b2]
    return out


_CACHE = {}


def kernel(**inputs):
    inputs = {k: np.asarray(v, np.float32) for k, v in inputs.items()}
    x = inputs["L0_attn_x"]
    xc = np.ascontiguousarray(np.transpose(x, (0, 2, 1)))

    if "ncA" not in _CACHE:
        _CACHE["ncA"] = build_launch_A()
        _CACHE["ncB"] = build_launch_B()
        _CACHE["ncC"] = build_launch_C()
    ncA, ncB, ncC = _CACHE["ncA"], _CACHE["ncB"], _CACHE["ncC"]

    wA = dict(
        wqT=np.ascontiguousarray(inputs["L0_attn_wq_w"].T),
        wkT=np.ascontiguousarray(inputs["L0_attn_wk_w"].T),
        wvT=np.ascontiguousarray(inputs["L0_attn_wv_w"].T),
        woT=np.ascontiguousarray(inputs["L0_attn_wo_w"].T),
        bq=np.ascontiguousarray(inputs["L0_attn_wq_b"].reshape(8, 128).T),
        bk=np.ascontiguousarray(inputs["L0_attn_wk_b"].reshape(8, 128).T),
        bv=np.ascontiguousarray(inputs["L0_attn_wv_b"].reshape(8, 128).T),
        bo=np.ascontiguousarray(inputs["L0_attn_wo_b"].reshape(8, 128).T),
    )
    inA = []
    for core in range(NCORE):
        b, q = core // 4, core % 4
        start = q * TLOC
        d = dict(xb=xc[b], xq=_slice_pad(xc[b], start - QH, start + TLOC))
        d.update(wA)
        inA.append({k: np.ascontiguousarray(v, np.float32) for k, v in d.items()})
    resA = run_bass_kernel_spmd(ncA, inA, core_ids=list(range(NCORE)))
    yfull = np.zeros((B, C, T), np.float32)
    a0full = np.zeros((B, C, T), np.float32)
    for core in range(NCORE):
        b, q = core // 4, core % 4
        start = q * TLOC
        yfull[b][:, start:start + TLOC] = np.asarray(
            resA.results[core]["xn0_out"], np.float32)[:, QH:]
        a0full[b][:, start:start + TLOC] = np.asarray(
            resA.results[core]["a0_out"], np.float32)

    mw, m1 = make_masks_B()
    TBP = TLOC + 384
    sharedB = dict(selq=make_selq(), selb=make_selb(), mask_w=mw, mask_m1=m1)
    tabs_pc, vmask_pc = [], []
    for core in range(NCORE):
        q = core % 4
        start = q * TLOC
        tabs_pc.append(make_rotary_tabs(T, start - 384, start + TLOC))
        pos = (start - 384 + 128 * np.arange(TBP // 128)[None, :]
               + np.arange(128)[:, None])
        vmask_pc.append((pos >= 0).astype(np.float32))

    for l in range(FFN_DEPTH):
        lay = prep_B_layer(
            inputs["L0_ffn_Wqkv"][l], inputs["L0_ffn_Wo"][l],
            inputs["L0_ffn_W1"][l], inputs["L0_ffn_b1"][l],
            inputs["L0_ffn_W2"][l], inputs["L0_ffn_b2"][l],
            inputs["L0_ffn_ln_g"][l], inputs["L0_ffn_ln_b"][l],
            inputs["L0_ffn_q_scale"][l], inputs["L0_ffn_k_scale"][l])
        inB = []
        for core in range(NCORE):
            b, q = core // 4, core % 4
            start = q * TLOC
            d = dict(y_in=_slice_pad(yfull[b], start - HB, start + TLOC),
                     tabs=tabs_pc[core], vmask=vmask_pc[core])
            d.update(sharedB)
            for k, v in lay.items():
                d[k + "0"] = v
            inB.append({k: np.ascontiguousarray(v, np.float32) for k, v in d.items()})
        resB = run_bass_kernel_spmd(ncB, inB, core_ids=list(range(NCORE)))
        newy = np.zeros((B, C, T), np.float32)
        for core in range(NCORE):
            b, q = core // 4, core % 4
            start = q * TLOC
            newy[b][:, start:start + TLOC] = np.asarray(
                resB.results[core]["y_out"], np.float32)
        yfull = newy

    wC = prep_C_weights(inputs)
    inC = []
    for core in range(NCORE):
        b, q = core // 4, core % 4
        start = q * TLOC
        d = dict(x0s=_slice_pad(xc[b], start - H1C, start + TLOC),
                 a0s=_slice_pad(a0full[b], start - H1C, start + TLOC),
                 f0s=_slice_pad(yfull[b], start - H1C, start + TLOC))
        d.update(wC)
        d.update(prep_C_consts(start))
        inC.append({k: np.ascontiguousarray(v, np.float32) for k, v in d.items()})
    resC = run_bass_kernel_spmd(ncC, inC, core_ids=list(range(NCORE)))
    outc = np.zeros((B, C, T), np.float32)
    for core in range(NCORE):
        b, q = core // 4, core % 4
        start = q * TLOC
        outc[b][:, start:start + TLOC] = np.asarray(
            resC.results[core]["out"], np.float32)
    return np.ascontiguousarray(np.transpose(outc, (0, 2, 1)))
